# version 13
# baseline (speedup 1.0000x reference)
import numpy as np
import ml_dtypes
import contextlib
import concourse.bass as bass
import concourse.mybir as mybir
from concourse.bass_utils import run_bass_kernel_spmd
import concourse.bass_utils as _bu

if not getattr(_bu, "_ldw_opt_patch", False):
    _bu._ldw_opt_patch = True
    _orig_run_command = _bu.run_command

    def _run_command_ldw(argv, **kwargs):
        argv = ["--enable-ldw-opt=true" if a == "--enable-ldw-opt=false" else a
                for a in argv]
        return _orig_run_command(argv, **kwargs)

    _bu.run_command = _run_command_ldw


F32 = mybir.dt.float32
F32R = mybir.dt.float32r
BF16 = mybir.dt.bfloat16
AF = mybir.ActivationFunctionType
ALU = mybir.AluOpType
AX = mybir.AxisListType

D = 1024
S = 2048
NCH = 8
NU = 16
NG = 4
UPG = 4
TG = 512
NE = 8
EPS = 1e-5
INV_D = 1.0 / D


def build_program(taps=(), stop_after=None):
    taps = set(taps)
    _stop = [stop_after]
    nc = bass.Bass("TRN2", target_bir_lowering=False, debug=False)

    x_d = nc.dram_tensor("x", [128, NCH * S], F32, kind="ExternalInput").ap()
    wq_d = nc.dram_tensor("wq", [128, NCH * D], F32, kind="ExternalInput").ap()
    wk_d = nc.dram_tensor("wk", [128, NCH * D], F32, kind="ExternalInput").ap()
    wv_d = nc.dram_tensor("wv", [128, NCH * D], F32, kind="ExternalInput").ap()
    wf_d = nc.dram_tensor("wf", [128, NCH * D], F32, kind="ExternalInput").ap()
    wmod_d = nc.dram_tensor("wmod", [6, 128, NCH * D], F32, kind="ExternalInput").ap()
    cT_d = nc.dram_tensor("cT", [D, NE], F32, kind="ExternalInput").ap()
    bq_d = nc.dram_tensor("bqr", [1, D], F32, kind="ExternalInput").ap()
    bk_d = nc.dram_tensor("bkr", [1, D], F32, kind="ExternalInput").ap()
    bv_d = nc.dram_tensor("bvr", [1, D], F32, kind="ExternalInput").ap()
    bf_d = nc.dram_tensor("bfr", [1, D], F32, kind="ExternalInput").ap()
    rwT_d = nc.dram_tensor("rwT", [D, NE], F32, kind="ExternalInput").ap()
    rb_d = nc.dram_tensor("rbr", [1, NE], F32, kind="ExternalInput").ap()
    ew_d = nc.dram_tensor("ew", [NE, 128, NCH * D], F32, kind="ExternalInput").ap()
    eb_d = nc.dram_tensor("eb", [NE, D], F32, kind="ExternalInput").ap()
    out_d = nc.dram_tensor("out", [D, S], F32, kind="ExternalOutput").ap()

    id_d = nc.inline_tensor(np.eye(128, dtype=np.float32), "identf")
    sel_np = np.zeros((NE, NE * 128), dtype=ml_dtypes.bfloat16)
    for _e in range(NE):
        sel_np[_e, _e * 128:(_e + 1) * 128] = 1.0
    sel_d = nc.inline_tensor(sel_np, "sel_const")

    tap_d = {}
    def deftap(name, shape, dtype=F32):
        if name in taps:
            tap_d[name] = nc.dram_tensor("tap_" + name, shape, dtype,
                                         kind="ExternalOutput").ap()
    deftap("mod", [128, 48]); deftap("h1", [128, NCH * TG], BF16)
    deftap("qg", [128, NCH * TG], BF16); deftap("kg", [128, NCH * TG], BF16)
    deftap("vaug", [128, UPG * 16 * 65], BF16)
    deftap("negmax", [128, 16]); deftap("kdup", [128, S], BF16)
    deftap("kaug", [128, S], BF16); deftap("qaug", [128, S], BF16)
    deftap("pt", [128, S], BF16)
    deftap("ctx2", [128, 1024], BF16); deftap("ctxn", [128, 1024], BF16)
    deftap("x2", [128, NCH * S], BF16); deftap("h2t", [128, NCH * S], BF16)
    deftap("ltm", [128, 128]); deftap("coef", [128, 128])

    es = contextlib.ExitStack()
    E = es.enter_context

    x_sb = E(nc.sbuf_tensor("x_sb", [128, NCH * S], BF16))
    mod_raw = E(nc.sbuf_tensor("mod_raw", [128, 48], F32))
    sa_t = E(nc.sbuf_tensor("sa_t", [128, 8], F32))
    ca1_t = E(nc.sbuf_tensor("ca1_t", [128, 8], F32))
    ga_t = E(nc.sbuf_tensor("ga_t", [128, 8], F32))
    sb_t = E(nc.sbuf_tensor("sb_t", [128, 8], F32))
    cb1_t = E(nc.sbuf_tensor("cb1_t", [128, 8], F32))
    gb_t = E(nc.sbuf_tensor("gb_t", [128, 8], F32))
    scr8 = E(nc.sbuf_tensor("scr8x", [128, 8], F32))
    scr8b = E(nc.sbuf_tensor("scr8b", [128, 8], F32))
    scr8u = E(nc.sbuf_tensor("scr8u", [128, 8], mybir.dt.uint8))
    rb_row = E(nc.sbuf_tensor("rb_row", [1, NE], BF16))
    rwT_sb = E(nc.sbuf_tensor("rwT_sb", [128, NCH * NE], BF16))
    eb_sb = E(nc.sbuf_tensor("eb_sb", [NE, D], BF16))
    ident = E(nc.sbuf_tensor("ident", [128, 128], F32))
    ones_col = E(nc.sbuf_tensor("ones_col", [128, 1], BF16))
    ones_row = E(nc.sbuf_tensor("ones_row", [1, 512], BF16))

    sems = {}
    for name in ("ld", "rep", "mm", "vec", "act", "out", "ew"):
        sems[name] = [E(nc.semaphore("s_" + name)), 0]

    class SemRef:
        __slots__ = ("key",)
        def __init__(self, key): self.key = key
        @property
        def h(self): return sems[self.key][0]
        @property
        def n(self): return sems[self.key][1]
        def inc(self, instr, k=1):
            instr.then_inc(self.h, k); sems[self.key][1] += k
            return sems[self.key][1]
    s_ld, s_rep, s_mm, s_vec, s_act, s_out, s_ew = (SemRef(k) for k in
        ("ld", "rep", "mm", "vec", "act", "out", "ew"))

    def rcp(ap):
        return ap.rearrange("(c p) n -> p c n", p=128)

    L = E
    mean_r = L(nc.sbuf_tensor("mean_r", [1, 512], F32))
    msq_r = L(nc.sbuf_tensor("msq_r", [1, 512], F32))
    var_r = L(nc.sbuf_tensor("var_r", [1, 512], F32))
    lnv_r = L(nc.sbuf_tensor("lnv_r", [1, 512], F32))
    rstd_bf = L(nc.sbuf_tensor("rstd_bf", [1, 512], BF16))
    nmr_bf = L(nc.sbuf_tensor("nmr_bf", [1, 512], BF16))
    cT_sb = L(nc.sbuf_tensor("cT_sb", [128, NCH * NE], F32))
    siluC = L(nc.sbuf_tensor("siluC", [128, NCH * NE], BF16))

    with nc.Block("b0") as block:
        @block.vector
        def _(v):
            v.memset(ones_col[:], 1.0)
            v.memset(ones_row[:], 1.0)
        @block.sync
        def _(sy):
            s_ld.inc(sy.dma_start(ident[:], id_d.ap()), 16)
            s_ld.inc(sy.dma_start(cT_sb[:], rcp(cT_d)), 16)
        @block.gpsimd
        def _(g):
            s_ld.inc(g.dma_start(x_sb[:], x_d), 16)
            s_ld.inc(g.dma_start(rb_row[:], rb_d), 16)
            s_ld.inc(g.dma_start(eb_sb[:], eb_d), 16)
            s_ld.inc(g.dma_start(rwT_sb[:], rcp(rwT_d)), 16)
    LD_SMALL = s_ld.n

    with contextlib.ExitStack() as wsc:
        wmod_sb = [wsc.enter_context(nc.sbuf_tensor(f"wmod{i}", [128, NCH * D], BF16))
                   for i in range(2)]
        modp_sb = wsc.enter_context(nc.sbuf_tensor("modp_sb", [NE, 6 * D], F32))
        with nc.Block("b1") as block:
            with (nc.psum_tensor("ps_mod0", [NE, D], F32) as ps_mod0,
                  nc.psum_tensor("ps_mod1", [NE, D], F32) as ps_mod1,
                  nc.psum_tensor("ps_tr", [128, 384], F32) as ps_tr):
                ps_mod = [ps_mod0, ps_mod1]
                mmb, actb = s_mm.n, s_act.n
                wmod_ld = []
                @block.gpsimd
                def _(g):
                    for p in range(6):
                        if p >= 2:
                            g.wait_ge(s_mm.h, mmb + 2 * (p - 2) + 1)
                        s_ld.inc(g.dma_start(wmod_sb[p % 2][:], wmod_d[p]), 16)
                        wmod_ld.append(s_ld.n)
                @block.scalar
                def _(sc):
                    sc.wait_ge(s_ld.h, LD_SMALL)
                    s_act.inc(sc.activation(siluC[:], cT_sb[:], AF.Silu))
                    for p in range(6):
                        sc.wait_ge(s_mm.h, mmb + 2 * p + 1)
                        s_act.inc(sc.activation(modp_sb[:, p * D:(p + 1) * D],
                                                ps_mod[p % 2][:], AF.Copy))
                    sc.wait_ge(s_mm.h, mmb + 12)
                    src = ps_tr[:].rearrange("p (j k) -> p j k", k=8)[:, :, 0:1]
                    s_act.inc(sc.copy(mod_raw[:], src))
                @block.tensor
                def _(t):
                    t.wait_ge(s_act.h, actb + 1)
                    for p in range(6):
                        t.wait_ge(s_ld.h, wmod_ld[p])
                        if p >= 2:
                            t.wait_ge(s_act.h, actb + 2 + (p - 2))
                        w = wmod_sb[p % 2]
                        last = None
                        for dc in range(NCH):
                            for h in range(2):
                                last = t.matmul(ps_mod[p % 2][:, h * 512:(h + 1) * 512],
                                         siluC[:, dc * NE:(dc + 1) * NE],
                                         w[:, dc * D + h * 512: dc * D + (h + 1) * 512],
                                         start=(dc == 0), stop=(dc == NCH - 1))
                        s_mm.inc(last)
                        t.wait_ge(s_act.h, actb + 2 + p)
                        for c in range(NCH):
                            last = t.transpose(ps_tr[:, (p * 8 + c) * 8:(p * 8 + c) * 8 + 8],
                                        modp_sb[0:NE, p * D + c * 128: p * D + (c + 1) * 128],
                                        ident[0:NE, 0:NE])
                        s_mm.inc(last)
                @block.vector
                def _(v):
                    v.wait_ge(s_act.h, actb + 8)
                    v.tensor_copy(sa_t[:], mod_raw[:, 0:8])
                    v.tensor_scalar(ca1_t[:], mod_raw[:, 8:16], 1.0, None, ALU.add)
                    v.tensor_copy(sb_t[:], mod_raw[:, 24:32])
                    v.tensor_scalar(cb1_t[:], mod_raw[:, 32:40], 1.0, None, ALU.add)
                    for gsrc, gdst in ((mod_raw[:, 16:24], ga_t), (mod_raw[:, 40:48], gb_t)):
                        v.tensor_scalar(scr8[:], gsrc, 0.25, None, ALU.max)
                        v.tensor_scalar(scr8b[:], gsrc, -0.25, None, ALU.min)
                        v.tensor_scalar(scr8u[:], gsrc, 0.0, None, ALU.is_ge)
                        s_vec.inc(v.select(gdst[:], scr8u[:], scr8[:], scr8b[:]))
                if "mod" in taps:
                    @block.sync
                    def _(sy):
                        sy.wait_ge(s_vec.h, s_vec.n)
                        s_out.inc(sy.dma_start(tap_d["mod"], mod_raw[:]), 16)
    MOD_DONE = s_vec.n

    def ln_group(name, g, src_sb, dst_sb, dst_is_group, c1_t, s_t, xsq_buf):
        t0 = g * TG
        with nc.Block(name) as block:
            with (nc.psum_tensor(name + "_stat", [1, 1024], F32) as ps_stat,
                  nc.psum_tensor(name + "_bc", [128, 1024], F32) as ps_bc):
                mmb, actb, vecb = s_mm.n, s_act.n, s_vec.n
                @block.scalar
                def _(sc):
                    sc.wait_ge(s_ld.h, LD_SMALL)
                    for c in range(NCH):
                        if c >= 2:
                            sc.wait_ge(s_mm.h, mmb + (c - 1))
                        s_act.inc(sc.activation(xsq_buf[:, (c % 2) * 512:(c % 2 + 1) * 512],
                                                src_sb[:, c * S + t0: c * S + t0 + TG],
                                                AF.Square))
                    sc.wait_ge(s_vec.h, vecb + 2)
                    s_act.inc(sc.activation(lnv_r[:], var_r[:], AF.Ln))
                    s_act.inc(sc.activation(rstd_bf[:], lnv_r[:], AF.Exp,
                                            bias=0.0, scale=-0.5))
                @block.tensor
                def _(t):
                    t.wait_ge(s_ld.h, LD_SMALL)
                    for c in range(NCH):
                        t.matmul(ps_stat[0:1, 0:512], ones_col[:],
                                 src_sb[:, c * S + t0: c * S + t0 + TG],
                                 start=(c == 0), stop=(c == NCH - 1))
                    for c in range(NCH):
                        t.wait_ge(s_act.h, actb + c + 1)
                        s_mm.inc(t.matmul(ps_stat[0:1, 512:1024], ones_col[:],
                                 xsq_buf[:, (c % 2) * 512:(c % 2 + 1) * 512],
                                 start=(c == 0), stop=(c == NCH - 1)))
                    t.wait_ge(s_vec.h, vecb + 3)
                    t.matmul(ps_bc[:, 0:512], ones_row[0:1, 0:128], rstd_bf[:],
                             start=True, stop=True)
                    s_mm.inc(t.matmul(ps_bc[:, 512:1024], ones_row[0:1, 0:128], nmr_bf[:],
                             start=True, stop=True))
                @block.vector
                def _(v):
                    v.wait_ge(s_mm.h, mmb + 8)
                    v.tensor_scalar(mean_r[:], ps_stat[0:1, 0:512], INV_D, None, ALU.mult)
                    v.tensor_mul(msq_r[:], mean_r[:], mean_r[:])
                    s_vec.inc(v.tensor_scalar(msq_r[:], msq_r[:], 1.0, -EPS,
                                              ALU.mult, ALU.add))
                    s_vec.inc(v.scalar_tensor_tensor(var_r[:], ps_stat[0:1, 512:1024],
                                                     INV_D, msq_r[:],
                                                     ALU.mult, ALU.subtract))
                    v.wait_ge(s_act.h, actb + 10)
                    s_vec.inc(v.scalar_tensor_tensor(nmr_bf[:], mean_r[:], -1.0,
                                                     rstd_bf[:], ALU.mult, ALU.mult))
                    v.wait_ge(s_mm.h, mmb + 9)
                    v.wait_ge(s_vec.h, MOD_DONE)
                    for c in range(NCH):
                        if dst_is_group:
                            dslc = dst_sb[:, c * TG: (c + 1) * TG]
                        else:
                            dslc = dst_sb[:, c * S + t0: c * S + t0 + TG]
                        v.tensor_tensor(dslc, src_sb[:, c * S + t0: c * S + t0 + TG],
                                        ps_bc[:, 0:512], ALU.mult)
                        v.tensor_tensor(dslc, dslc, ps_bc[:, 512:1024], ALU.add)
                        s_vec.inc(v.tensor_scalar(dslc, dslc, c1_t[:, c:c + 1],
                                                  s_t[:, c:c + 1], ALU.mult, ALU.add))

    att_scope = contextlib.ExitStack()
    AE = att_scope.enter_context
    wq_sb = AE(nc.sbuf_tensor("wq_sb", [128, NCH * D], BF16))
    wk_sb = AE(nc.sbuf_tensor("wk_sb", [128, NCH * D], BF16))
    wv_sb = AE(nc.sbuf_tensor("wv_sb", [128, NCH * D], BF16))
    wf_sb = AE(nc.sbuf_tensor("wf_sb", [128, NCH * D], BF16))
    bq_row = AE(nc.sbuf_tensor("bq_row", [1, D], BF16))
    bk_row = AE(nc.sbuf_tensor("bk_row", [1, D], BF16))
    bv_row = AE(nc.sbuf_tensor("bv_row", [1, D], BF16))
    bf_row = AE(nc.sbuf_tensor("bf_row", [1, D], BF16))
    h1g = AE(nc.sbuf_tensor("h1g", [128, NCH * TG], BF16))
    qg = AE(nc.sbuf_tensor("qg", [128, NCH * TG], BF16))
    kg = AE(nc.sbuf_tensor("kg", [128, NCH * TG], BF16))
    vaug = AE(nc.sbuf_tensor("vaug", [128, UPG * 16 * 65 + 64], BF16))
    xsq = AE(nc.sbuf_tensor("xsq", [128, 1024], BF16))
    kdA = AE(nc.sbuf_tensor("kdA", [128, S], BF16))
    kdB = AE(nc.sbuf_tensor("kdB", [128, S], BF16))
    k_aug = AE(nc.sbuf_tensor("k_aug", [128, S], BF16))
    q_aug = AE(nc.sbuf_tensor("q_aug", [128, S], BF16))
    Pt = AE(nc.sbuf_tensor("Pt", [128, S], BF16))
    negmax = AE(nc.sbuf_tensor("negmax", [128, 16], F32))
    flat_src = AE(nc.sbuf_tensor("flat_src", [16, 128], F32))
    invZ_bf = AE(nc.sbuf_tensor("invZ_bf", [1, S], BF16))
    bz_sb = Pt[0:64, :]
    pvctx = Pt[64:128, :]
    ctx2u = AE(nc.sbuf_tensor("ctx2u", [128, 1024], BF16))
    ctxsq = xsq

    if stop_after != "mod":
        with nc.Block("battinit") as block:
            @block.gpsimd
            def _(g):
                for w_sb, w_dd in ((wq_sb, wq_d), (wk_sb, wk_d), (wv_sb, wv_d), (wf_sb, wf_d)):
                    s_ld.inc(g.dma_start(w_sb[:], w_dd), 16)
                for dst, src in ((bq_row, bq_d), (bk_row, bk_d), (bv_row, bv_d), (bf_row, bf_d)):
                    s_ld.inc(g.dma_start(dst[:], src), 16)
            @block.vector
            def _(v):
                if stop_after != "battnomemset":
                    v.memset(vaug[:], 0.0)
                    va = vaug[:, 0:UPG * 16 * 65].rearrange(
                        "p (u f j) -> p u f j", f=16, j=65)
                    v.memset(va[:, :, :, 64:65], 1.0)
                    v.memset(k_aug[64:128, :], 0.0)
                    v.memset(k_aug[64:65, :], 1.0)
                    v.memset(q_aug[64:128, :], 0.0)
                    v.memset(kdA[64:128, :], 0.0)
                    v.memset(kdB[0:64, :], 0.0)
    LD_W = s_ld.n

    def qkv_group(g):
        t0 = g * TG
        with nc.Block(f"qkv_{g}") as block:
            with (nc.psum_tensor(f"ps_q0_{g}", [128, 512], F32) as ps_q0,
                  nc.psum_tensor(f"ps_q1_{g}", [128, 512], F32) as ps_q1,
                  nc.psum_tensor(f"ps_v0_{g}", [128, 1024], F32) as ps_v0,
                  nc.psum_tensor(f"ps_v1_{g}", [128, 1024], F32) as ps_v1):
                ps_q = [ps_q0, ps_q1]
                ps_v = [ps_v0, ps_v1]
                mmb, actb, vecb = s_mm.n, s_act.n, s_vec.n
                @block.tensor
                def _(t):
                    t.wait_ge(s_ld.h, LD_W)
                    for wi, (w_sb, b_row) in enumerate(((wq_sb, bq_row), (wk_sb, bk_row))):
                        for oc in range(NCH):
                            idx = wi * 8 + oc
                            if idx >= 2:
                                t.wait_ge(s_act.h, actb + idx - 1)
                            p = ps_q[idx % 2]
                            for dc in range(NCH):
                                t.matmul(p[:], w_sb[:, dc * D + oc * 128: dc * D + (oc + 1) * 128],
                                         h1g[:, dc * TG: (dc + 1) * TG],
                                         start=(dc == 0), stop=False)
                            s_mm.inc(t.matmul(p[:], b_row[0:1, oc * 128:(oc + 1) * 128],
                                     ones_row[0:1, :], start=False, stop=True))
                    for tc in range(4):
                        if tc >= 2:
                            t.wait_ge(s_vec.h, vecb + tc - 1)
                        p = ps_v[tc % 2]
                        for dc in range(NCH):
                            lhs = h1g[:, dc * TG + tc * 128: dc * TG + (tc + 1) * 128]
                            for h in range(2):
                                t.matmul(p[:, h * 512:(h + 1) * 512], lhs,
                                         wv_sb[:, dc * D + h * 512: dc * D + (h + 1) * 512],
                                         start=(dc == 0), stop=False)
                        last = None
                        for h in range(2):
                            last = t.matmul(p[:, h * 512:(h + 1) * 512], ones_row[0:1, 0:128],
                                     bv_row[0:1, h * 512:(h + 1) * 512],
                                     start=False, stop=True)
                        s_mm.inc(last)
                @block.scalar
                def _(sc):
                    for wi, dst in enumerate((qg, kg)):
                        for oc in range(NCH):
                            idx = wi * 8 + oc
                            sc.wait_ge(s_mm.h, mmb + idx + 1)
                            s_act.inc(sc.activation(dst[:, oc * TG:(oc + 1) * TG],
                                                    ps_q[idx % 2][:], AF.Copy,
                                                    bias=0.0,
                                                    scale=(8.0 if wi == 0 else 1.0)))
                @block.vector
                def _(v):
                    va = vaug[:, 0:UPG * 16 * 65].rearrange(
                        "p (u f j) -> p u f j", f=16, j=65)
                    for tc in range(4):
                        v.wait_ge(s_mm.h, mmb + 16 + tc + 1)
                        src = ps_v[tc % 2][:].rearrange("p (f j) -> p f j", j=64)
                        s_vec.inc(v.tensor_copy(va[:, tc, :, 0:64], src))
                if g == 0 and any(nm in taps for nm in ("h1", "qg", "kg", "vaug")):
                    @block.sync
                    def _(sy):
                        sy.wait_ge(s_act.h, s_act.n)
                        sy.wait_ge(s_vec.h, s_vec.n)
                        for nm, bufap in (("h1", h1g[:]), ("qg", qg[:]), ("kg", kg[:]),
                                          ("vaug", vaug[:, 0:UPG * 16 * 65])):
                            if nm in taps:
                                s_out.inc(sy.dma_start(tap_d[nm], bufap), 16)

            u = g * UPG
            go = 0
            mm_q = s_mm.n
            act_q = s_act.n
            vec_q = s_vec.n
            with (nc.psum_tensor(f"psA_{u}", [128, S], F32) as psA,
                  nc.psum_tensor(f"psB_{u}", [128, S], F32) as psB):
                @block.gpsimd
                def _(g_):
                    g_.wait_ge(s_act.h, act_q)
                    emit_repacks(g_, u)
                    g_.wait_ge(s_act.h, act_q + 1)
                    s_rep.inc(g_.dma_start(q_aug[64:65, :], flat_src[:]), 16)
                    M[("flat", u)] = s_rep.n
                @block.tensor
                def _(t):
                    t.wait_ge(s_rep.h, M[("kdup", u)])
                    t.wait_ge(s_vec.h, vec_q)
                    for j in range(NCH):
                        if j >= 1:
                            t.wait_ge(s_vec.h, vec_q + 2 * j)
                        lqj = qg[0:128, j * TG + go: j * TG + go + 128]
                        for n in range(4):
                            t.matmul(psA[:, n * 512:(n + 1) * 512], lqj,
                                     kdA[0:128, n * 512:(n + 1) * 512],
                                     start=True, stop=True)
                        last = None
                        for n in range(4):
                            last = t.matmul(psB[:, n * 512:(n + 1) * 512], lqj,
                                     kdB[0:128, n * 512:(n + 1) * 512],
                                     start=True, stop=True)
                        s_mm.inc(last)
                @block.vector
                def _(v):
                    for j in range(NCH):
                        v.wait_ge(s_mm.h, mm_q + j + 1)
                        s_vec.inc(v.reduce_max(negmax[:, 2 * j:2 * j + 1], psA[:],
                                               axis=AX.X, negate=True))
                        s_vec.inc(v.reduce_max(negmax[:, 2 * j + 1:2 * j + 2], psB[:],
                                               axis=AX.X, negate=True))
                    M[("reds", u)] = s_vec.n
            with nc.psum_tensor(f"ps_nm_{u}", [16, 128], F32) as ps_nm:
                mmb2 = s_mm.n
                @block.tensor
                def _(t):
                    t.wait_ge(s_vec.h, M[("reds", u)])
                    s_mm.inc(t.transpose(ps_nm[:], negmax[:], ident[:]))
                @block.scalar
                def _(sc):
                    sc.wait_ge(s_mm.h, mmb2 + 1)
                    s_act.inc(sc.copy(flat_src[:], ps_nm[:]))
                    M[("flatevac", u)] = s_act.n

    zt = AE(nc.sbuf_tensor("zt", [128, 16], F32))
    zt2 = AE(nc.sbuf_tensor("zt2", [128, 16], F32))
    negmaxA = AE(nc.sbuf_tensor("negmaxA", [128, 16], F32))
    negmaxB = AE(nc.sbuf_tensor("negmaxB", [128, 16], F32))
    M = {}

    def emit_repacks(g_, u):
        g = u // UPG
        go = (u % UPG) * 128
        kdAv = kdA[:].rearrange("p (c g2 t) -> p c g2 t", g2=2, t=128)
        kdBv = kdB[:].rearrange("p (c g2 t) -> p c g2 t", g2=2, t=128)
        kav = k_aug[:].rearrange("p (c g2 t) -> p c g2 t", g2=2, t=128)
        qav = q_aug[:].rearrange("p (c g2 t) -> p c g2 t", g2=2, t=128)
        kgv_t = kg[0:64, :].rearrange("p (c t) -> p c t", t=TG)[:, :, go:go + 128]
        kgv_b = kg[64:128, :].rearrange("p (c t) -> p c t", t=TG)[:, :, go:go + 128]
        qgv_t = qg[0:64, :].rearrange("p (c t) -> p c t", t=TG)[:, :, go:go + 128]
        qgv_b = qg[64:128, :].rearrange("p (c t) -> p c t", t=TG)[:, :, go:go + 128]
        for dst, srcv in ((kdAv[0:64, :, 0, :], kgv_t), (kdAv[0:64, :, 1, :], kgv_b),
                          (kdBv[64:128, :, 0, :], kgv_t), (kdBv[64:128, :, 1, :], kgv_b)):
            s_rep.inc(g_.dma_start(dst, srcv), 16)
        M[("kdup", u)] = s_rep.n
        for dst, srcv in ((kav[0:64, :, 0, :], kgv_t), (kav[0:64, :, 1, :], kgv_b),
                          (qav[0:64, :, 0, :], qgv_t), (qav[0:64, :, 1, :], qgv_b)):
            s_rep.inc(g_.dma_start(dst, srcv), 16)


    pv_all = AE(nc.sbuf_tensor("pv_all", [65, UPG * S], BF16))
    ctxn_g = AE(nc.sbuf_tensor("ctxn_g", [128, NCH * TG], BF16))

    def attz_block(block, g, uo):
        u = g * UPG + uo
        nxt = u + 1 if uo < UPG - 1 else None
        ngo = (uo + 1) * 128
        if True:
            with (nc.psum_tensor(f"zpv_{u}", [128, S], F32) as ps_PV,
                  nc.psum_tensor(f"zst_{u}", [128, 1024], F32) as ps_St,
                  nc.psum_tensor(f"zs_{u}", [128, 1024], F32) as ps_Sz):
                mm0, act0, vec0 = s_mm.n, s_act.n, s_vec.n
                per_f = 6 if nxt is not None else 4
                def mm_st(f, h):   return mm0 + per_f * f + (1 if h == 0 else (3 if nxt is not None else 2))
                def mm_sz(f, h):   return mm0 + per_f * f + (2 if h == 0 else 4)
                def mm_pv(f, h):   return mm0 + per_f * f + (per_f - 1 if h == 0 else per_f)
                def act_exp(f, h): return act0 + 2 * f + h + 1
                def vec_red(f, h): return vec0 + 2 * f + h + 1
                tE = mm0 + 16 * per_f
                if nxt is not None:
                    M[("nmtr", nxt)] = tE + 1
                    M[("reds", nxt)] = vec0 + 33
                M[("st15", u)] = mm_st(15, 1)
                if nxt is not None:
                    M[("szlast", u)] = mm_sz(15, 1)
                M[("pvdone", u)] = mm_pv(15, 1)
                M[("expdone", u)] = act0 + 32
                M[("pvevac", u)] = act0 + 33
                if nxt is not None:
                    M[("flatevac", nxt)] = act0 + 34

                @block.gpsimd
                def _(g_):
                    if nxt is not None:
                        if uo >= 1:
                            g_.wait_ge(s_mm.h, M[("szlast", u - 1)])
                        kdAv = kdA[:].rearrange("p (c g2 t) -> p c g2 t", g2=2, t=128)
                        kdBv = kdB[:].rearrange("p (c g2 t) -> p c g2 t", g2=2, t=128)
                        kgv_t = kg[0:64, :].rearrange("p (c t) -> p c t", t=TG)[:, :, ngo:ngo + 128]
                        kgv_b = kg[64:128, :].rearrange("p (c t) -> p c t", t=TG)[:, :, ngo:ngo + 128]
                        for dst, srcv in ((kdAv[0:64, :, 0, :], kgv_t), (kdAv[0:64, :, 1, :], kgv_b),
                                          (kdBv[64:128, :, 0, :], kgv_t), (kdBv[64:128, :, 1, :], kgv_b)):
                            s_rep.inc(g_.dma_start(dst, srcv), 16)
                        M[("kdup", nxt)] = s_rep.n
                        g_.wait_ge(s_mm.h, M[("st15", u)])
                        kav = k_aug[:].rearrange("p (c g2 t) -> p c g2 t", g2=2, t=128)
                        qav = q_aug[:].rearrange("p (c g2 t) -> p c g2 t", g2=2, t=128)
                        qgv_t = qg[0:64, :].rearrange("p (c t) -> p c t", t=TG)[:, :, ngo:ngo + 128]
                        qgv_b = qg[64:128, :].rearrange("p (c t) -> p c t", t=TG)[:, :, ngo:ngo + 128]
                        for dst, srcv in ((kav[0:64, :, 0, :], kgv_t), (kav[0:64, :, 1, :], kgv_b),
                                          (qav[0:64, :, 0, :], qgv_t), (qav[0:64, :, 1, :], qgv_b)):
                            s_rep.inc(g_.dma_start(dst, srcv), 16)
                        g_.wait_ge(s_act.h, M[("flatevac", nxt)])
                        s_rep.inc(g_.dma_start(q_aug[64:65, :], flat_src[:]), 16)
                        M[("flat", nxt)] = s_rep.n

                @block.tensor
                def _(t):
                    t.wait_ge(s_rep.h, M[("flat", u)])
                    t.wait_ge(s_act.h, M.get(("expdone", u - 1), 0))
                    for f in range(16):
                        for h in range(2):
                            if f >= 1 or h == 1:
                                t.wait_ge(s_act.h, act_exp(f - 1 + h, 1 - h))
                            last = None
                            for n in range(2):
                                rq = h * 1024 + n * 512
                                last = t.matmul(ps_St[:, n * 512:(n + 1) * 512],
                                         k_aug[0:128, f * 128:(f + 1) * 128],
                                         q_aug[0:128, rq:rq + 512],
                                         start=True, stop=True)
                            s_mm.inc(last)
                            if nxt is not None:
                                if f >= 1 or h == 1:
                                    t.wait_ge(s_vec.h, vec_red(f - 1 + h, 1 - h))
                                if f == 0 and h == 0:
                                    t.wait_ge(s_rep.h, M[("kdup", nxt)])
                                    t.wait_ge(s_act.h, M[("flatevac", u)])
                                kd = kdA if f % 2 == 0 else kdB
                                lq = qg[0:128, (f // 2) * TG + ngo: (f // 2) * TG + ngo + 128]
                                last = None
                                for n in range(2):
                                    rp = h * 1024 + n * 512
                                    last = t.matmul(ps_Sz[:, n * 512:(n + 1) * 512], lq,
                                             kd[0:128, rp:rp + 512],
                                             start=True, stop=True)
                                s_mm.inc(last)
                        for h in range(2):
                            t.wait_ge(s_act.h, act_exp(f, h))
                            if f == 0 and h == 0:
                                t.wait_ge(s_act.h, M.get(("pvevac", u - 1), 0))
                            last = None
                            for n in range(2):
                                rq = h * 1024 + n * 512
                                last = t.matmul(ps_PV[:, rq:rq + 512],
                                         vaug[:, uo * 1040 + f * 65: uo * 1040 + f * 65 + 128],
                                         Pt[:, rq:rq + 512],
                                         start=(f == 0), stop=(f == 15))
                            s_mm.inc(last)
                    if nxt is not None:
                        t.wait_ge(s_vec.h, M[("reds", nxt)])
                        s_mm.inc(t.transpose(ps_Sz[0:16, 512:640], negmax[:], ident[:]))

                @block.scalar
                def _(sc):
                    for f in range(16):
                        for h in range(2):
                            sc.wait_ge(s_mm.h, mm_st(f, h))
                            if f == 0 and h == 0:
                                sc.wait_ge(s_mm.h, M.get(("pvdone", u - 1), 0))
                            if f >= 1:
                                sc.wait_ge(s_mm.h, mm_pv(f - 1, h))
                            s_act.inc(sc.activation(Pt[:, h * 1024:(h + 1) * 1024],
                                                    ps_St[:], AF.Exp))
                    sc.wait_ge(s_mm.h, M[("pvdone", u)])
                    s_act.inc(sc.activation(pv_all[:, uo * S:(uo + 1) * S],
                                            ps_PV[0:65, :], AF.Copy))
                    if nxt is not None:
                        sc.wait_ge(s_mm.h, M[("nmtr", nxt)])
                        s_act.inc(sc.copy(flat_src[:], ps_Sz[0:16, 512:640]))

                @block.vector
                def _(v):
                    if nxt is not None:
                        for f in range(16):
                            for h in range(2):
                                v.wait_ge(s_mm.h, mm_sz(f, h))
                                tgt = negmaxA if h == 0 else negmaxB
                                s_vec.inc(v.reduce_max(tgt[:, f:f + 1], ps_Sz[:],
                                                       axis=AX.X, negate=True))
                        v.tensor_copy(scr8[:], negmaxA[:, 0:8])
                        s_vec.inc(v.tensor_tensor(negmax[:], negmaxA[:], negmaxB[:],
                                                  ALU.min))
                        assert s_vec.n == M[("reds", nxt)]

    def attepi_block(block, g):
        if True:
            with (nc.psum_tensor(f"ebz_{g}", [64, S], F32) as ps_bz,
                  nc.psum_tensor(f"eln_{g}", [128, 512], F32) as ps_ln,
                  nc.psum_tensor(f"ef0_{g}", [128, 512], F32) as ps_f0,
                  nc.psum_tensor(f"ef1_{g}", [128, 512], F32) as ps_f1,
                  nc.psum_tensor(f"ewm_{g}", [128, 512], F32) as ps_warm):
                ps_F = [ps_f0, ps_f1]
                mm0, act0, vec0, rep0 = s_mm.n, s_act.n, s_vec.n, s_rep.n
                A_ctxsq = [act0 + 2 * i + 1 for i in range(UPG)]
                A_rstd = [act0 + 2 * i + 2 for i in range(UPG)]
                T_bz = [mm0 + 3 * i + 1 for i in range(UPG)]
                T_ln = [mm0 + 3 * i + 2 for i in range(UPG)]
                T_b2 = [mm0 + 3 * i + 3 for i in range(UPG)]
                T_f = [mm0 + 3 * UPG + oc + 1 for oc in range(NCH)]
                V_rec = [vec0 + 7 * i + 1 for i in range(UPG)]
                V_bz = [vec0 + 7 * i + 2 for i in range(UPG)]
                V_pvm = [vec0 + 7 * i + 3 for i in range(UPG)]
                V_ctx2 = [vec0 + 7 * i + 4 for i in range(UPG)]
                V_var = [vec0 + 7 * i + 5 for i in range(UPG)]
                V_nmr = [vec0 + 7 * i + 6 for i in range(UPG)]
                V_ctxn = [vec0 + 7 * i + 7 for i in range(UPG)]
                V_fev = [vec0 + 7 * UPG + oc + 1 for oc in range(NCH)]
                R_z1 = [rep0 + 64 * i + 16 for i in range(UPG)]
                R_invz = [rep0 + 64 * i + 32 for i in range(UPG)]
                R_remap = [rep0 + 64 * i + 64 for i in range(UPG)]

                @block.scalar
                def _(sc):
                    for i in range(UPG):
                        sc.wait_ge(s_vec.h, V_ctx2[i])
                        s_act.inc(sc.activation(ctxsq[:], ctx2u[:], AF.Square))
                        sc.wait_ge(s_vec.h, V_var[i])
                        sc.activation(lnv_r[0:1, 0:128], var_r[0:1, 0:128], AF.Ln)
                        s_act.inc(sc.activation(rstd_bf[0:1, 0:128], lnv_r[0:1, 0:128],
                                                AF.Exp, bias=0.0, scale=-0.5))

                @block.tensor
                def _(t):
                    def warm(n):
                        for _ in range(n):
                            t.matmul(ps_warm[:], wf_sb[:, 0:128],
                                     h1g[:, 0:512], start=True, stop=True)
                    t.wait_ge(s_act.h, M[("expdone", g * UPG + UPG - 1)])
                    t.wait_ge(s_act.h, M[("pvevac", g * UPG + UPG - 1)])
                    for i in range(UPG):
                        warm(12)
                        t.wait_ge(s_rep.h, R_invz[i])
                        if i >= 1:
                            t.wait_ge(s_vec.h, V_bz[i - 1])
                        last = None
                        for n in range(4):
                            last = t.matmul(ps_bz[:, n * 512:(n + 1) * 512],
                                     ones_row[0:1, 0:64],
                                     invZ_bf[0:1, n * 512:(n + 1) * 512],
                                     start=True, stop=True)
                        s_mm.inc(last)
                        warm(12)
                        t.wait_ge(s_vec.h, V_ctx2[i])
                        if i >= 1:
                            t.wait_ge(s_vec.h, V_ctxn[i - 1])
                        for c in range(NCH):
                            t.matmul(ps_ln[0:1, 0:128], ones_col[:],
                                     ctx2u[:, c * 128:(c + 1) * 128],
                                     start=(c == 0), stop=(c == NCH - 1))
                        warm(4)
                        t.wait_ge(s_act.h, A_ctxsq[i])
                        last = None
                        for c in range(NCH):
                            last = t.matmul(ps_ln[0:1, 128:256], ones_col[:],
                                     ctxsq[:, c * 128:(c + 1) * 128],
                                     start=(c == 0), stop=(c == NCH - 1))
                        s_mm.inc(last)
                        warm(6)
                        t.wait_ge(s_vec.h, V_nmr[i])
                        t.matmul(ps_ln[:, 256:384], ones_row[0:1, 0:128],
                                 rstd_bf[0:1, 0:128], start=True, stop=True)
                        s_mm.inc(t.matmul(ps_ln[:, 384:512], ones_row[0:1, 0:128],
                                 nmr_bf[0:1, 0:128], start=True, stop=True))
                    t.wait_ge(s_ld.h, LD_W)
                    for oc in range(NCH):
                        p = ps_F[oc % 2]
                        if oc == 0:
                            warm(16)
                        t.wait_ge(s_vec.h, V_ctxn[UPG - 1])
                        if oc >= 2:
                            warm(6)
                            t.wait_ge(s_vec.h, V_fev[oc - 2])
                        for dc in range(NCH):
                            t.matmul(p[:], wf_sb[:, dc * D + oc * 128: dc * D + (oc + 1) * 128],
                                     ctxn_g[:, dc * TG:(dc + 1) * TG],
                                     start=(dc == 0), stop=False)
                        s_mm.inc(t.matmul(p[:], bf_row[0:1, oc * 128:(oc + 1) * 128],
                                 ones_row[0:1, :], start=False, stop=True))

                @block.vector
                def _(v):
                    for i in range(UPG):
                        u = g * UPG + i
                        go_i = i * 128
                        v.wait_ge(s_rep.h, R_z1[i])
                        with nc.allow_low_precision(reason="1/Z softmax scale via bf16 round-trip, within tolerance"):
                            s_vec.inc(v.reciprocal(zt2[:], zt[:]))
                        v.wait_ge(s_mm.h, T_bz[i])
                        s_vec.inc(v.tensor_copy(bz_sb[:], ps_bz[:]))
                        if i >= 1:
                            v.wait_ge(s_rep.h, R_remap[i - 1])
                        s_vec.inc(v.tensor_mul(pvctx[:], pv_all[0:64, i * S:(i + 1) * S],
                                               bz_sb[:]))
                        v.wait_ge(s_rep.h, R_remap[i])
                        h1v = h1g[:].rearrange("p (c t) -> p c t", t=TG)[:, :, go_i:go_i + 128]
                        c2v = ctx2u[:].rearrange("p (c t) -> p c t", t=128)
                        s_vec.inc(v.tensor_tensor(c2v, c2v, h1v, ALU.add))
                        v.wait_ge(s_mm.h, T_ln[i])
                        v.tensor_scalar(mean_r[0:1, 0:128], ps_ln[0:1, 0:128],
                                        INV_D, None, ALU.mult)
                        v.tensor_mul(msq_r[0:1, 0:128], mean_r[0:1, 0:128],
                                     mean_r[0:1, 0:128])
                        v.tensor_scalar(msq_r[0:1, 0:128], msq_r[0:1, 0:128],
                                        1.0, -EPS, ALU.mult, ALU.add)
                        s_vec.inc(v.scalar_tensor_tensor(var_r[0:1, 0:128],
                                                         ps_ln[0:1, 128:256], INV_D,
                                                         msq_r[0:1, 0:128],
                                                         ALU.mult, ALU.subtract))
                        v.wait_ge(s_act.h, A_rstd[i])
                        s_vec.inc(v.scalar_tensor_tensor(nmr_bf[0:1, 0:128],
                                                         mean_r[0:1, 0:128], -1.0,
                                                         rstd_bf[0:1, 0:128],
                                                         ALU.mult, ALU.mult))
                        v.wait_ge(s_mm.h, T_b2[i])
                        for c in range(NCH):
                            v.tensor_tensor(ctxn_g[:, c * TG + go_i: c * TG + go_i + 128],
                                            ctx2u[:, c * 128:(c + 1) * 128],
                                            ps_ln[:, 256:384], ALU.mult)
                            ins = v.tensor_tensor(ctxn_g[:, c * TG + go_i: c * TG + go_i + 128],
                                                  ctxn_g[:, c * TG + go_i: c * TG + go_i + 128],
                                                  ps_ln[:, 384:512], ALU.add)
                            if c == NCH - 1:
                                s_vec.inc(ins)
                    for oc in range(NCH):
                        v.wait_ge(s_mm.h, T_f[oc])
                        xs = x_sb[:, oc * S + g * TG: oc * S + (g + 1) * TG]
                        s_vec.inc(v.scalar_tensor_tensor(
                            xs, ps_F[oc % 2][:], ga_t[:, oc:oc + 1], xs,
                            ALU.mult, ALU.add))

                @block.gpsimd
                def _(g_):
                    for i in range(UPG):
                        u = g * UPG + i
                        if i >= 1:
                            g_.wait_ge(s_vec.h, V_rec[i - 1])
                        g_.wait_ge(s_act.h, M[("pvevac", u)])
                        s_rep.inc(g_.dma_start(zt[:], pv_all[64:65, i * S:(i + 1) * S]), 16)
                        g_.wait_ge(s_vec.h, V_rec[i])
                        if i >= 1:
                            g_.wait_ge(s_mm.h, T_bz[i - 1])
                        s_rep.inc(g_.dma_start(invZ_bf[:], zt2[:]), 16)
                        g_.wait_ge(s_vec.h, V_pvm[i])
                        if i >= 1:
                            g_.wait_ge(s_vec.h, V_ctxn[i - 1])
                        pv_e = pvctx[:].rearrange("p (c f2 t) -> p c f2 t", f2=2, t=128)
                        c2 = ctx2u[:].rearrange("p (c t) -> p c t", t=128)
                        s_rep.inc(g_.dma_start(c2[0:64, :, :], pv_e[:, :, 0, :]), 16)
                        s_rep.inc(g_.dma_start(c2[64:128, :, :], pv_e[:, :, 1, :]), 16)

                if g == 0 and any(nm in taps for nm in ("ctx2", "ctxn")):
                    @block.sync
                    def _(sy):
                        if "ctx2" in taps:
                            sy.wait_ge(s_vec.h, V_ctx2[0])
                            s_out.inc(sy.dma_start(tap_d["ctx2"], ctx2u[:]), 16)
                        if "ctxn" in taps:
                            sy.wait_ge(s_vec.h, V_ctxn[0])
                            s_out.inc(sy.dma_start(tap_d["ctxn"], ctxn_g[:, 0:1024]), 16)


    if stop_after not in ("mod", "battinit", "battnomemset"):
        for g in range(NG):
            ln_group(f"ln1_{g}", g, x_sb, h1g, True, ca1_t, sa_t, xsq)
            if stop_after == "ln1":
                break
            qkv_group(g)
            if stop_after == "qkv":
                break
            with nc.Block(f"attg_{g}") as ab:
                for uo in range(UPG):
                    attz_block(ab, g, uo)
                attepi_block(ab, g)
            if stop_after in ("unit0", "alpha0"):
                break

    att_scope.close()
    _skip_moe = stop_after in ("mod", "battinit", "battnomemset", "ln1", "qkv", "unit0", "alpha0", "att")

    if "x2" in taps:
        with nc.Block("tapx2") as block:
            @block.sync
            def _(sy):
                s_out.inc(sy.dma_start(tap_d["x2"], x_sb[:]), 16)

    moe_pre = contextlib.ExitStack()
    MP = moe_pre.enter_context
    h2 = E(nc.sbuf_tensor("h2", [128, NCH * S], BF16))
    coefT_bf = E(nc.sbuf_tensor("coefT_bf", [NE, S], BF16))
    sel_sb = E(nc.sbuf_tensor("sel_sb", [NE, NE * 128], BF16))
    xsq2 = MP(nc.sbuf_tensor("xsq2", [128, 1024], BF16))
    logits_s = MP(nc.sbuf_tensor("logits_s", [NE, S], F32))
    logits_tm = MP(nc.sbuf_tensor("logits_tm", [128, 128], F32))
    coef_tm = MP(nc.sbuf_tensor("coef_tm", [128, 128], F32))
    nm1 = MP(nc.sbuf_tensor("nm1", [128, 1], F32))
    z1 = MP(nc.sbuf_tensor("z1", [128, 1], F32))
    w1 = MP(nc.sbuf_tensor("w1", [128, 1], F32))
    scr_e = MP(nc.sbuf_tensor("scr_e", [128, 8], F32))
    scr_ge = MP(nc.sbuf_tensor("scr_ge", [128, 8], F32))
    scr_cu = MP(nc.sbuf_tensor("scr_cu", [128, 8], F32))
    scr_eq = MP(nc.sbuf_tensor("scr_eq", [128, 8], F32))
    scr_m = MP(nc.sbuf_tensor("scr_m", [128, 8], F32))

    if not _skip_moe:
     for g in range(NG):
        ln_group(f"ln3_{g}", g, x_sb, h2, False, cb1_t, sb_t, xsq2)

    if "h2t" in taps:
        with nc.Block("taph2") as block:
            @block.sync
            def _(sy):
                s_out.inc(sy.dma_start(tap_d["h2t"], h2[:]), 16)

    if not _skip_moe:
     with nc.Block("router") as block:
        with (nc.psum_tensor("ps_lg", [NE, S], F32) as ps_lg,
              nc.psum_tensor("ps_ltm", [128, 128], F32) as ps_ltm):
            mmb, actb = s_mm.n, s_act.n
            @block.tensor
            def _(t):
                t.wait_ge(s_ld.h, LD_SMALL)
                for n in range(4):
                    for dc in range(NCH):
                        t.matmul(ps_lg[:, n * 512:(n + 1) * 512],
                                 rwT_sb[:, dc * NE:(dc + 1) * NE],
                                 h2[:, dc * S + n * 512: dc * S + (n + 1) * 512],
                                 start=(dc == 0), stop=False)
                    last = t.matmul(ps_lg[:, n * 512:(n + 1) * 512], rb_row[:],
                             ones_row[0:1, :], start=False, stop=True)
                s_mm.inc(last)
                t.wait_ge(s_act.h, actb + 1)
                for tch in range(16):
                    last = t.transpose(ps_ltm[:, tch * 8:(tch + 1) * 8],
                                logits_s[0:NE, tch * 128:(tch + 1) * 128],
                                ident[0:NE, 0:NE])
                s_mm.inc(last)
            @block.scalar
            def _(sc):
                sc.wait_ge(s_mm.h, mmb + 1)
                s_act.inc(sc.activation(logits_s[:], ps_lg[:], AF.Copy))
                sc.wait_ge(s_mm.h, mmb + 2)
                s_act.inc(sc.copy(logits_tm[:], ps_ltm[:]))
            if "ltm" in taps:
                @block.sync
                def _(sy):
                    sy.wait_ge(s_act.h, s_act.n)
                    s_out.inc(sy.dma_start(tap_d["ltm"], logits_tm[:]), 16)

    if not _skip_moe:
     with nc.Block("coef") as block:
        vecb, actb = s_vec.n, s_act.n
        @block.vector
        def _(v):
            for tch in range(16):
                sl = logits_tm[:, tch * 8:(tch + 1) * 8]
                if tch >= 1:
                    v.wait_ge(s_act.h, actb + tch)
                s_vec.inc(v.reduce_max(nm1[:], sl, axis=AX.X, negate=True))
                v.wait_ge(s_act.h, actb + tch + 1)
                s_vec.inc(v.reciprocal(w1[:], z1[:]))
                v.tensor_scalar(scr_ge[:], sl, nm1[:], 0.0, ALU.add, ALU.is_ge)
                v.tensor_tensor_scan(scr_cu[:], scr_ge[:], scr_ge[:], 0.0,
                                     ALU.add, ALU.bypass)
                v.tensor_scalar(scr_eq[:], scr_cu[:], 1.0, None, ALU.is_equal)
                v.tensor_mul(scr_m[:], scr_ge[:], scr_eq[:])
                s_vec.inc(v.tensor_scalar(coef_tm[:, tch * 8:(tch + 1) * 8],
                                          scr_m[:], w1[:], None, ALU.mult))
        @block.scalar
        def _(sc):
            for tch in range(16):
                sc.wait_ge(s_vec.h, vecb + 3 * tch + 1)
                if tch >= 1:
                    sc.wait_ge(s_vec.h, vecb + 3 * (tch - 1) + 2)
                s_act.inc(sc.activation(scr_e[:], logits_tm[:, tch * 8:(tch + 1) * 8],
                                        AF.Exp, bias=nm1[:], scale=1.0,
                                        accum_out=z1[:]))
        if "coef" in taps:
            @block.sync
            def _(sy):
                sy.wait_ge(s_vec.h, s_vec.n)
                s_out.inc(sy.dma_start(tap_d["coef"], coef_tm[:]), 16)

    if not _skip_moe:
     with nc.Block("coefT") as block:
        with nc.psum_tensor("ps_cT", [NE, S], F32) as ps_cT:
            mmb = s_mm.n
            @block.tensor
            def _(t):
                last = None
                for tch in range(16):
                    last = t.transpose(ps_cT[0:NE, tch * 128:(tch + 1) * 128],
                                coef_tm[:, tch * 8:(tch + 1) * 8], ident[:])
                s_mm.inc(last)
            @block.scalar
            def _(sc):
                sc.wait_ge(s_mm.h, mmb + 1)
                s_act.inc(sc.activation(coefT_bf[:], ps_cT[:], AF.Copy))
        @block.sync
        def _(sy):
            s_ld.inc(sy.dma_start(sel_sb[:], sel_d.ap()), 16)
    moe_pre.close()

    moe_scope = contextlib.ExitStack()
    ME = moe_scope.enter_context
    Be_tc = ME(nc.sbuf_tensor("Be_tc", [128, NE * 512], BF16))
    ew_sb = [ME(nc.sbuf_tensor(f"ew_sb{i}", [128, NCH * D], BF16)) for i in range(2)]
    h2s = [ME(nc.sbuf_tensor(f"h2s{i}", [128, NCH * 512], BF16)) for i in range(2)]
    out_stage = ME(nc.sbuf_tensor("out_stage", [128, 4096], F32))

    if not _skip_moe:
     with nc.Block("experts") as block:
        with nc.psum_tensor("ps_moe", [128, 4096], F32) as ps_moe:
            mmb, vecb, actb, ewb, outb = s_mm.n, s_vec.n, s_act.n, s_ew.n, s_out.n
            @block.gpsimd
            def _(g):
                for tc in range(4):
                    for e in range(NE):
                        i = tc * NE + e
                        if i >= 2:
                            i2 = i - 2
                            g.wait_ge(s_mm.h, mmb + (i2 // NE) * 17 + 8 + (i2 % NE) + 1)
                        s_ew.inc(g.dma_start(ew_sb[i % 2][:], ew_d[e]), 16)
            @block.sync
            def _(sy):
                for tc in range(4):
                    for oc in range(NCH):
                        sy.wait_ge(s_vec.h, vecb + tc * 24 + 16 + oc + 1)
                        s_out.inc(sy.dma_start(
                            out_d[oc * 128:(oc + 1) * 128, tc * 512:(tc + 1) * 512],
                            out_stage[:, oc * 512:(oc + 1) * 512]), 16)
            @block.tensor
            def _(t):
                t.wait_ge(s_ld.h, s_ld.n)
                for tc in range(4):
                    for e in range(NE):
                        slot = (e % 2) * 512
                        if e >= 2:
                            t.wait_ge(s_vec.h, vecb + tc * 24 + e - 1)
                        if tc >= 1 and e < 2:
                            t.wait_ge(s_vec.h, vecb + (tc - 1) * 24 + 24)
                        s_mm.inc(t.matmul(ps_moe[:, slot:slot + 512],
                                 sel_sb[0:NE, e * 128:(e + 1) * 128],
                                 coefT_bf[0:NE, tc * 512:(tc + 1) * 512],
                                 start=True, stop=True))
                    for e in range(NE):
                        t.wait_ge(s_ew.h, ewb + (tc * NE + e + 1) * 16)
                        t.wait_ge(s_vec.h, vecb + tc * 24 + 8 + e + 1)
                        buf = ew_sb[(tc * NE + e) % 2]
                        hbuf = h2s[e % 2]
                        last = None
                        for oc in range(NCH):
                            for dc in range(NCH):
                                last = t.matmul(ps_moe[:, oc * 512:(oc + 1) * 512],
                                         buf[:, dc * D + oc * 128: dc * D + (oc + 1) * 128],
                                         hbuf[:, dc * 512:(dc + 1) * 512],
                                         start=(e == 0 and dc == 0), stop=False)
                        s_mm.inc(last)
                    last = None
                    for oc in range(NCH):
                        last = t.matmul(ps_moe[:, oc * 512:(oc + 1) * 512],
                                 eb_sb[0:NE, oc * 128:(oc + 1) * 128],
                                 coefT_bf[0:NE, tc * 512:(tc + 1) * 512],
                                 start=False, stop=True)
                    s_mm.inc(last)
            @block.vector
            def _(v):
                for tc in range(4):
                    for e in range(NE):
                        slot = (e % 2) * 512
                        v.wait_ge(s_mm.h, mmb + tc * 17 + e + 1)
                        s_vec.inc(v.tensor_copy(Be_tc[:, e * 512:(e + 1) * 512],
                                                ps_moe[:, slot:slot + 512]))
                    for e in range(NE):
                        hbuf = h2s[e % 2]
                        if e >= 2:
                            v.wait_ge(s_mm.h, mmb + tc * 17 + 8 + e - 1)
                        for dc in range(NCH):
                            ins = v.tensor_tensor(
                                hbuf[:, dc * 512:(dc + 1) * 512],
                                h2[:, dc * S + tc * 512: dc * S + (tc + 1) * 512],
                                Be_tc[:, e * 512:(e + 1) * 512], ALU.mult)
                            if dc == NCH - 1:
                                s_vec.inc(ins)
                    v.wait_ge(s_mm.h, mmb + tc * 17 + 17)
                    if tc >= 1:
                        v.wait_ge(s_out.h, outb + tc * NCH * 16)
                    for oc in range(NCH):
                        xs = x_sb[:, oc * S + tc * 512: oc * S + (tc + 1) * 512]
                        s_vec.inc(v.scalar_tensor_tensor(
                            out_stage[:, oc * 512:(oc + 1) * 512],
                            ps_moe[:, oc * 512:(oc + 1) * 512],
                            gb_t[:, oc:oc + 1], xs, ALU.mult, ALU.add))
    moe_scope.close()

    with nc.Block("fin") as block:
        @block.sync
        def _(sy):
            sy.wait_ge(s_out.h, s_out.n)

    es.close()
    nc.finalize()
    return nc, sorted(tap_d.keys())


_cache = {}

def _get_program(taps=()):
    key = tuple(sorted(taps))
    if key not in _cache:
        _cache[key] = build_program(taps=key)
    return _cache[key]


def make_in_maps(inputs):
    f32 = np.float32
    def cm(w):
        w = np.asarray(w, f32)
        return np.ascontiguousarray(w.reshape(NCH, 128, w.shape[1]).transpose(1, 0, 2)
                                    .reshape(128, NCH * w.shape[1]))
    x = np.asarray(inputs["L0_attn_x"], f32)
    C = np.asarray(inputs["L0_c"], f32)
    wmod0 = np.asarray(inputs["L0_wmod"], f32).reshape(D, 6, D).transpose(1, 0, 2)
    wmod = np.ascontiguousarray(
        wmod0.reshape(6, NCH, 128, D).transpose(0, 2, 1, 3).reshape(6, 128, NCH * D))
    ew0 = np.asarray(inputs["L0_ffn_expert_w"], f32).transpose(0, 2, 1)
    ew = np.ascontiguousarray(
        ew0.reshape(NE, NCH, 128, D).transpose(0, 2, 1, 3).reshape(NE, 128, NCH * D))
    shared = {
        "wq": cm(inputs["L0_attn_Wq"]),
        "wk": cm(inputs["L0_attn_Wk"]),
        "wv": cm(inputs["L0_attn_Wv"]),
        "wf": cm(inputs["L0_attn_Wf"]),
        "wmod": wmod,
        "bqr": np.ascontiguousarray(np.asarray(inputs["L0_attn_bq"], f32).reshape(1, D)),
        "bkr": np.ascontiguousarray(np.asarray(inputs["L0_attn_bk"], f32).reshape(1, D)),
        "bvr": np.ascontiguousarray(np.asarray(inputs["L0_attn_bv"], f32).reshape(1, D)),
        "bfr": np.ascontiguousarray(np.asarray(inputs["L0_attn_bf"], f32).reshape(1, D)),
        "rwT": np.ascontiguousarray(np.asarray(inputs["L0_ffn_router_w"], f32).T),
        "rbr": np.ascontiguousarray(np.asarray(inputs["L0_ffn_router_b"], f32).reshape(1, NE)),
        "ew": ew,
        "eb": np.ascontiguousarray(np.asarray(inputs["L0_ffn_expert_b"], f32)),
    }
    in_maps = []
    for b in range(8):
        order = [b] + [i for i in range(8) if i != b]
        m = dict(shared)
        xb = x[b].T
        m["x"] = np.ascontiguousarray(xb.reshape(NCH, 128, S).transpose(1, 0, 2)
                                      .reshape(128, NCH * S))
        m["cT"] = np.ascontiguousarray(C[order].T)
        in_maps.append(m)
    return in_maps


def kernel(**inputs):
    nc, _ = _get_program()
    in_maps = make_in_maps(inputs)
    res = run_bass_kernel_spmd(nc, in_maps, core_ids=list(range(8)))
    out = np.stack([np.asarray(r["out"]).T for r in res.results], axis=0)
    return out.astype(np.float32)


# version 14
# speedup vs baseline: 1.0208x; 1.0208x over previous
import numpy as np
import ml_dtypes
import contextlib
import concourse.bass as bass
import concourse.mybir as mybir
from concourse.bass_utils import run_bass_kernel_spmd
import concourse.bass_utils as _bu

if not getattr(_bu, "_ldw_opt_patch", False):
    _bu._ldw_opt_patch = True
    _orig_run_command = _bu.run_command

    def _run_command_ldw(argv, **kwargs):
        argv = ["--enable-ldw-opt=true" if a == "--enable-ldw-opt=false" else a
                for a in argv]
        return _orig_run_command(argv, **kwargs)

    _bu.run_command = _run_command_ldw


F32 = mybir.dt.float32
F32R = mybir.dt.float32r
BF16 = mybir.dt.bfloat16
AF = mybir.ActivationFunctionType
ALU = mybir.AluOpType
AX = mybir.AxisListType

D = 1024
S = 2048
NCH = 8
NU = 16
NG = 4
UPG = 4
TG = 512
NE = 8
EPS = 1e-5
INV_D = 1.0 / D


def build_program(taps=(), stop_after=None):
    taps = set(taps)
    _stop = [stop_after]
    nc = bass.Bass("TRN2", target_bir_lowering=False, debug=False)

    x_d = nc.dram_tensor("x", [128, NCH * S], F32, kind="ExternalInput").ap()
    wq_d = nc.dram_tensor("wq", [128, NCH * D], F32, kind="ExternalInput").ap()
    wk_d = nc.dram_tensor("wk", [128, NCH * D], F32, kind="ExternalInput").ap()
    wv_d = nc.dram_tensor("wv", [128, NCH * D], F32, kind="ExternalInput").ap()
    wf_d = nc.dram_tensor("wf", [128, NCH * D], F32, kind="ExternalInput").ap()
    wmod_d = nc.dram_tensor("wmod", [6, 128, NCH * D], F32, kind="ExternalInput").ap()
    cT_d = nc.dram_tensor("cT", [D, NE], F32, kind="ExternalInput").ap()
    bq_d = nc.dram_tensor("bqr", [1, D], F32, kind="ExternalInput").ap()
    bk_d = nc.dram_tensor("bkr", [1, D], F32, kind="ExternalInput").ap()
    bv_d = nc.dram_tensor("bvr", [1, D], F32, kind="ExternalInput").ap()
    bf_d = nc.dram_tensor("bfr", [1, D], F32, kind="ExternalInput").ap()
    rwT_d = nc.dram_tensor("rwT", [D, NE], F32, kind="ExternalInput").ap()
    rb_d = nc.dram_tensor("rbr", [1, NE], F32, kind="ExternalInput").ap()
    ew_d = nc.dram_tensor("ew", [NE, 128, NCH * D], F32, kind="ExternalInput").ap()
    eb_d = nc.dram_tensor("eb", [NE, D], F32, kind="ExternalInput").ap()
    out_d = nc.dram_tensor("out", [D, S], F32, kind="ExternalOutput").ap()

    id_d = nc.inline_tensor(np.eye(128, dtype=np.float32), "identf")
    sel_np = np.zeros((NE, NE * 128), dtype=ml_dtypes.bfloat16)
    for _e in range(NE):
        sel_np[_e, _e * 128:(_e + 1) * 128] = 1.0
    sel_d = nc.inline_tensor(sel_np, "sel_const")

    tap_d = {}
    def deftap(name, shape, dtype=F32):
        if name in taps:
            tap_d[name] = nc.dram_tensor("tap_" + name, shape, dtype,
                                         kind="ExternalOutput").ap()
    deftap("mod", [128, 48]); deftap("h1", [128, NCH * TG], BF16)
    deftap("qg", [128, NCH * TG], BF16); deftap("kg", [128, NCH * TG], BF16)
    deftap("vaug", [128, UPG * 16 * 65], BF16)
    deftap("negmax", [128, 16]); deftap("kdup", [128, S], BF16)
    deftap("kaug", [128, S], BF16); deftap("qaug", [128, S], BF16)
    deftap("pt", [128, S], BF16)
    deftap("ctx2", [128, 1024], BF16); deftap("ctxn", [128, 1024], BF16)
    deftap("x2", [128, NCH * S], BF16); deftap("h2t", [128, NCH * S], BF16)
    deftap("ltm", [128, 128]); deftap("coef", [128, 128])

    es = contextlib.ExitStack()
    E = es.enter_context

    x_sb = E(nc.sbuf_tensor("x_sb", [128, NCH * S], BF16))
    mod_raw = E(nc.sbuf_tensor("mod_raw", [128, 48], F32))
    sa_t = E(nc.sbuf_tensor("sa_t", [128, 8], F32))
    ca1_t = E(nc.sbuf_tensor("ca1_t", [128, 8], F32))
    ga_t = E(nc.sbuf_tensor("ga_t", [128, 8], F32))
    sb_t = E(nc.sbuf_tensor("sb_t", [128, 8], F32))
    cb1_t = E(nc.sbuf_tensor("cb1_t", [128, 8], F32))
    gb_t = E(nc.sbuf_tensor("gb_t", [128, 8], F32))
    scr8 = E(nc.sbuf_tensor("scr8x", [128, 8], F32))
    scr8b = E(nc.sbuf_tensor("scr8b", [128, 8], F32))
    scr8u = E(nc.sbuf_tensor("scr8u", [128, 8], mybir.dt.uint8))
    rb_row = E(nc.sbuf_tensor("rb_row", [1, NE], BF16))
    rwT_sb = E(nc.sbuf_tensor("rwT_sb", [128, NCH * NE], BF16))
    eb_sb = E(nc.sbuf_tensor("eb_sb", [NE, D], BF16))
    ident = E(nc.sbuf_tensor("ident", [128, 128], F32))
    ones_col = E(nc.sbuf_tensor("ones_col", [128, 1], BF16))
    ones_row = E(nc.sbuf_tensor("ones_row", [1, 512], BF16))

    sems = {}
    for name in ("ld", "rep", "mm", "vec", "act", "out", "ew"):
        sems[name] = [E(nc.semaphore("s_" + name)), 0]

    class SemRef:
        __slots__ = ("key",)
        def __init__(self, key): self.key = key
        @property
        def h(self): return sems[self.key][0]
        @property
        def n(self): return sems[self.key][1]
        def inc(self, instr, k=1):
            instr.then_inc(self.h, k); sems[self.key][1] += k
            return sems[self.key][1]
    s_ld, s_rep, s_mm, s_vec, s_act, s_out, s_ew = (SemRef(k) for k in
        ("ld", "rep", "mm", "vec", "act", "out", "ew"))

    def rcp(ap):
        return ap.rearrange("(c p) n -> p c n", p=128)

    L = E
    mean_r = L(nc.sbuf_tensor("mean_r", [1, 512], F32))
    msq_r = L(nc.sbuf_tensor("msq_r", [1, 512], F32))
    var_r = L(nc.sbuf_tensor("var_r", [1, 512], F32))
    lnv_r = L(nc.sbuf_tensor("lnv_r", [1, 512], F32))
    rstd_bf = L(nc.sbuf_tensor("rstd_bf", [1, 512], BF16))
    nmr_bf = L(nc.sbuf_tensor("nmr_bf", [1, 512], BF16))
    cT_sb = L(nc.sbuf_tensor("cT_sb", [128, NCH * NE], F32))
    siluC = L(nc.sbuf_tensor("siluC", [128, NCH * NE], BF16))

    with nc.Block("b0") as block:
        @block.vector
        def _(v):
            v.memset(ones_col[:], 1.0)
            v.memset(ones_row[:], 1.0)
        @block.sync
        def _(sy):
            s_ld.inc(sy.dma_start(ident[:], id_d.ap()), 16)
            s_ld.inc(sy.dma_start(cT_sb[:], rcp(cT_d)), 16)
        @block.gpsimd
        def _(g):
            s_ld.inc(g.dma_start(x_sb[:], x_d), 16)
            s_ld.inc(g.dma_start(rb_row[:], rb_d), 16)
            s_ld.inc(g.dma_start(eb_sb[:], eb_d), 16)
            s_ld.inc(g.dma_start(rwT_sb[:], rcp(rwT_d)), 16)
    LD_SMALL = s_ld.n

    with contextlib.ExitStack() as wsc:
        wmod_sb = [wsc.enter_context(nc.sbuf_tensor(f"wmod{i}", [128, NCH * D], BF16))
                   for i in range(2)]
        modp_sb = wsc.enter_context(nc.sbuf_tensor("modp_sb", [NE, 6 * D], F32))
        with nc.Block("b1") as block:
            with (nc.psum_tensor("ps_mod0", [NE, D], F32) as ps_mod0,
                  nc.psum_tensor("ps_mod1", [NE, D], F32) as ps_mod1,
                  nc.psum_tensor("ps_tr", [128, 384], F32) as ps_tr):
                ps_mod = [ps_mod0, ps_mod1]
                mmb, actb = s_mm.n, s_act.n
                wmod_ld = []
                @block.gpsimd
                def _(g):
                    for p in range(6):
                        if p >= 2:
                            g.wait_ge(s_mm.h, mmb + 2 * (p - 2) + 1)
                        s_ld.inc(g.dma_start(wmod_sb[p % 2][:], wmod_d[p]), 16)
                        wmod_ld.append(s_ld.n)
                @block.scalar
                def _(sc):
                    sc.wait_ge(s_ld.h, LD_SMALL)
                    s_act.inc(sc.activation(siluC[:], cT_sb[:], AF.Silu))
                    for p in range(6):
                        sc.wait_ge(s_mm.h, mmb + 2 * p + 1)
                        s_act.inc(sc.activation(modp_sb[:, p * D:(p + 1) * D],
                                                ps_mod[p % 2][:], AF.Copy))
                    sc.wait_ge(s_mm.h, mmb + 12)
                    src = ps_tr[:].rearrange("p (j k) -> p j k", k=8)[:, :, 0:1]
                    s_act.inc(sc.copy(mod_raw[:], src))
                @block.tensor
                def _(t):
                    t.wait_ge(s_act.h, actb + 1)
                    for p in range(6):
                        t.wait_ge(s_ld.h, wmod_ld[p])
                        if p >= 2:
                            t.wait_ge(s_act.h, actb + 2 + (p - 2))
                        w = wmod_sb[p % 2]
                        last = None
                        for dc in range(NCH):
                            for h in range(2):
                                last = t.matmul(ps_mod[p % 2][:, h * 512:(h + 1) * 512],
                                         siluC[:, dc * NE:(dc + 1) * NE],
                                         w[:, dc * D + h * 512: dc * D + (h + 1) * 512],
                                         start=(dc == 0), stop=(dc == NCH - 1))
                        s_mm.inc(last)
                        t.wait_ge(s_act.h, actb + 2 + p)
                        for c in range(NCH):
                            last = t.transpose(ps_tr[:, (p * 8 + c) * 8:(p * 8 + c) * 8 + 8],
                                        modp_sb[0:NE, p * D + c * 128: p * D + (c + 1) * 128],
                                        ident[0:NE, 0:NE])
                        s_mm.inc(last)
                @block.vector
                def _(v):
                    v.wait_ge(s_act.h, actb + 8)
                    v.tensor_copy(sa_t[:], mod_raw[:, 0:8])
                    v.tensor_scalar(ca1_t[:], mod_raw[:, 8:16], 1.0, None, ALU.add)
                    v.tensor_copy(sb_t[:], mod_raw[:, 24:32])
                    v.tensor_scalar(cb1_t[:], mod_raw[:, 32:40], 1.0, None, ALU.add)
                    for gsrc, gdst in ((mod_raw[:, 16:24], ga_t), (mod_raw[:, 40:48], gb_t)):
                        v.tensor_scalar(scr8[:], gsrc, 0.25, None, ALU.max)
                        v.tensor_scalar(scr8b[:], gsrc, -0.25, None, ALU.min)
                        v.tensor_scalar(scr8u[:], gsrc, 0.0, None, ALU.is_ge)
                        s_vec.inc(v.select(gdst[:], scr8u[:], scr8[:], scr8b[:]))
                if "mod" in taps:
                    @block.sync
                    def _(sy):
                        sy.wait_ge(s_vec.h, s_vec.n)
                        s_out.inc(sy.dma_start(tap_d["mod"], mod_raw[:]), 16)
    MOD_DONE = s_vec.n

    def ln_group(name, g, src_sb, dst_sb, dst_is_group, c1_t, s_t, xsq_buf):
        t0 = g * TG
        with nc.Block(name) as block:
            with (nc.psum_tensor(name + "_stat", [1, 1024], F32) as ps_stat,
                  nc.psum_tensor(name + "_bc", [128, 1024], F32) as ps_bc):
                mmb, actb, vecb = s_mm.n, s_act.n, s_vec.n
                @block.scalar
                def _(sc):
                    sc.wait_ge(s_ld.h, LD_SMALL)
                    for c in range(NCH):
                        if c >= 2:
                            sc.wait_ge(s_mm.h, mmb + (c - 1))
                        s_act.inc(sc.activation(xsq_buf[:, (c % 2) * 512:(c % 2 + 1) * 512],
                                                src_sb[:, c * S + t0: c * S + t0 + TG],
                                                AF.Square))
                    sc.wait_ge(s_vec.h, vecb + 2)
                    s_act.inc(sc.activation(lnv_r[:], var_r[:], AF.Ln))
                    s_act.inc(sc.activation(rstd_bf[:], lnv_r[:], AF.Exp,
                                            bias=0.0, scale=-0.5))
                @block.tensor
                def _(t):
                    t.wait_ge(s_ld.h, LD_SMALL)
                    for c in range(NCH):
                        t.matmul(ps_stat[0:1, 0:512], ones_col[:],
                                 src_sb[:, c * S + t0: c * S + t0 + TG],
                                 start=(c == 0), stop=(c == NCH - 1))
                    for c in range(NCH):
                        t.wait_ge(s_act.h, actb + c + 1)
                        s_mm.inc(t.matmul(ps_stat[0:1, 512:1024], ones_col[:],
                                 xsq_buf[:, (c % 2) * 512:(c % 2 + 1) * 512],
                                 start=(c == 0), stop=(c == NCH - 1)))
                    t.wait_ge(s_vec.h, vecb + 3)
                    t.matmul(ps_bc[:, 0:512], ones_row[0:1, 0:128], rstd_bf[:],
                             start=True, stop=True)
                    s_mm.inc(t.matmul(ps_bc[:, 512:1024], ones_row[0:1, 0:128], nmr_bf[:],
                             start=True, stop=True))
                @block.vector
                def _(v):
                    v.wait_ge(s_mm.h, mmb + 8)
                    v.tensor_scalar(mean_r[:], ps_stat[0:1, 0:512], INV_D, None, ALU.mult)
                    v.tensor_mul(msq_r[:], mean_r[:], mean_r[:])
                    s_vec.inc(v.tensor_scalar(msq_r[:], msq_r[:], 1.0, -EPS,
                                              ALU.mult, ALU.add))
                    s_vec.inc(v.scalar_tensor_tensor(var_r[:], ps_stat[0:1, 512:1024],
                                                     INV_D, msq_r[:],
                                                     ALU.mult, ALU.subtract))
                    v.wait_ge(s_act.h, actb + 10)
                    s_vec.inc(v.scalar_tensor_tensor(nmr_bf[:], mean_r[:], -1.0,
                                                     rstd_bf[:], ALU.mult, ALU.mult))
                    v.wait_ge(s_mm.h, mmb + 9)
                    v.wait_ge(s_vec.h, MOD_DONE)
                    for c in range(NCH):
                        if dst_is_group:
                            dslc = dst_sb[:, c * TG: (c + 1) * TG]
                        else:
                            dslc = dst_sb[:, c * S + t0: c * S + t0 + TG]
                        v.tensor_tensor(dslc, src_sb[:, c * S + t0: c * S + t0 + TG],
                                        ps_bc[:, 0:512], ALU.mult)
                        v.tensor_tensor(dslc, dslc, ps_bc[:, 512:1024], ALU.add)
                        s_vec.inc(v.tensor_scalar(dslc, dslc, c1_t[:, c:c + 1],
                                                  s_t[:, c:c + 1], ALU.mult, ALU.add))

    att_scope = contextlib.ExitStack()
    AE = att_scope.enter_context
    wq_sb = AE(nc.sbuf_tensor("wq_sb", [128, NCH * D], BF16))
    wk_sb = AE(nc.sbuf_tensor("wk_sb", [128, NCH * D], BF16))
    wv_sb = AE(nc.sbuf_tensor("wv_sb", [128, NCH * D], BF16))
    wf_sb = AE(nc.sbuf_tensor("wf_sb", [128, NCH * D], BF16))
    bq_row = AE(nc.sbuf_tensor("bq_row", [1, D], BF16))
    bk_row = AE(nc.sbuf_tensor("bk_row", [1, D], BF16))
    bv_row = AE(nc.sbuf_tensor("bv_row", [1, D], BF16))
    bf_row = AE(nc.sbuf_tensor("bf_row", [1, D], BF16))
    h1g = AE(nc.sbuf_tensor("h1g", [128, NCH * TG], BF16))
    qg = AE(nc.sbuf_tensor("qg", [128, NCH * TG], BF16))
    kg = AE(nc.sbuf_tensor("kg", [128, NCH * TG], BF16))
    vaug = AE(nc.sbuf_tensor("vaug", [128, UPG * 16 * 65 + 64], BF16))
    xsq = AE(nc.sbuf_tensor("xsq", [128, 1024], BF16))
    kdA = AE(nc.sbuf_tensor("kdA", [128, S], BF16))
    kdB = AE(nc.sbuf_tensor("kdB", [128, S], BF16))
    k_aug = AE(nc.sbuf_tensor("k_aug", [128, S], BF16))
    q_aug = AE(nc.sbuf_tensor("q_aug", [128, S], BF16))
    Pt = AE(nc.sbuf_tensor("Pt", [128, S], BF16))
    negmax = AE(nc.sbuf_tensor("negmax", [128, 16], F32))
    flat_src = AE(nc.sbuf_tensor("flat_src", [16, 128], F32))
    invZ_bf = AE(nc.sbuf_tensor("invZ_bf", [1, S], BF16))
    bz_sb = Pt[0:64, :]
    pvctx = Pt[64:128, :]
    ctx2u = AE(nc.sbuf_tensor("ctx2u", [128, 1024], BF16))
    ctxsq = xsq

    if stop_after != "mod":
        with nc.Block("battinit") as block:
            @block.gpsimd
            def _(g):
                for w_sb, w_dd in ((wq_sb, wq_d), (wk_sb, wk_d), (wv_sb, wv_d), (wf_sb, wf_d)):
                    s_ld.inc(g.dma_start(w_sb[:], w_dd), 16)
                for dst, src in ((bq_row, bq_d), (bk_row, bk_d), (bv_row, bv_d), (bf_row, bf_d)):
                    s_ld.inc(g.dma_start(dst[:], src), 16)
            @block.vector
            def _(v):
                if stop_after != "battnomemset":
                    v.memset(vaug[:], 0.0)
                    va = vaug[:, 0:UPG * 16 * 65].rearrange(
                        "p (u f j) -> p u f j", f=16, j=65)
                    v.memset(va[:, :, :, 64:65], 1.0)
                    v.memset(k_aug[64:128, :], 0.0)
                    v.memset(k_aug[64:65, :], 1.0)
                    v.memset(q_aug[64:128, :], 0.0)
                    v.memset(kdA[64:128, :], 0.0)
                    v.memset(kdB[0:64, :], 0.0)
    LD_W = s_ld.n

    def qkv_group(g):
        t0 = g * TG
        with nc.Block(f"qkv_{g}") as block:
            with (nc.psum_tensor(f"ps_q0_{g}", [128, 512], F32) as ps_q0,
                  nc.psum_tensor(f"ps_q1_{g}", [128, 512], F32) as ps_q1,
                  nc.psum_tensor(f"ps_v0_{g}", [128, 1024], F32) as ps_v0,
                  nc.psum_tensor(f"ps_v1_{g}", [128, 1024], F32) as ps_v1):
                ps_q = [ps_q0, ps_q1]
                ps_v = [ps_v0, ps_v1]
                mmb, actb, vecb = s_mm.n, s_act.n, s_vec.n
                @block.tensor
                def _(t):
                    t.wait_ge(s_ld.h, LD_W)
                    for wi, (w_sb, b_row) in enumerate(((wq_sb, bq_row), (wk_sb, bk_row))):
                        for oc in range(NCH):
                            idx = wi * 8 + oc
                            if idx >= 2:
                                t.wait_ge(s_act.h, actb + idx - 1)
                            p = ps_q[idx % 2]
                            for dc in range(NCH):
                                t.matmul(p[:], w_sb[:, dc * D + oc * 128: dc * D + (oc + 1) * 128],
                                         h1g[:, dc * TG: (dc + 1) * TG],
                                         start=(dc == 0), stop=False)
                            s_mm.inc(t.matmul(p[:], b_row[0:1, oc * 128:(oc + 1) * 128],
                                     ones_row[0:1, :], start=False, stop=True))
                    for tc in range(4):
                        if tc >= 2:
                            t.wait_ge(s_vec.h, vecb + tc - 1)
                        p = ps_v[tc % 2]
                        for dc in range(NCH):
                            lhs = h1g[:, dc * TG + tc * 128: dc * TG + (tc + 1) * 128]
                            for h in range(2):
                                t.matmul(p[:, h * 512:(h + 1) * 512], lhs,
                                         wv_sb[:, dc * D + h * 512: dc * D + (h + 1) * 512],
                                         start=(dc == 0), stop=False)
                        last = None
                        for h in range(2):
                            last = t.matmul(p[:, h * 512:(h + 1) * 512], ones_row[0:1, 0:128],
                                     bv_row[0:1, h * 512:(h + 1) * 512],
                                     start=False, stop=True)
                        s_mm.inc(last)
                @block.scalar
                def _(sc):
                    for wi, dst in enumerate((qg, kg)):
                        for oc in range(NCH):
                            idx = wi * 8 + oc
                            sc.wait_ge(s_mm.h, mmb + idx + 1)
                            s_act.inc(sc.activation(dst[:, oc * TG:(oc + 1) * TG],
                                                    ps_q[idx % 2][:], AF.Copy,
                                                    bias=0.0,
                                                    scale=(8.0 if wi == 0 else 1.0)))
                @block.vector
                def _(v):
                    va = vaug[:, 0:UPG * 16 * 65].rearrange(
                        "p (u f j) -> p u f j", f=16, j=65)
                    for tc in range(4):
                        v.wait_ge(s_mm.h, mmb + 16 + tc + 1)
                        src = ps_v[tc % 2][:].rearrange("p (f j) -> p f j", j=64)
                        s_vec.inc(v.tensor_copy(va[:, tc, :, 0:64], src))
                if g == 0 and any(nm in taps for nm in ("h1", "qg", "kg", "vaug")):
                    @block.sync
                    def _(sy):
                        sy.wait_ge(s_act.h, s_act.n)
                        sy.wait_ge(s_vec.h, s_vec.n)
                        for nm, bufap in (("h1", h1g[:]), ("qg", qg[:]), ("kg", kg[:]),
                                          ("vaug", vaug[:, 0:UPG * 16 * 65])):
                            if nm in taps:
                                s_out.inc(sy.dma_start(tap_d[nm], bufap), 16)

            u = g * UPG
            go = 0
            mm_q = s_mm.n
            act_q = s_act.n
            vec_q = s_vec.n
            with (nc.psum_tensor(f"psA_{u}", [128, S], F32) as psA,
                  nc.psum_tensor(f"psB_{u}", [128, S], F32) as psB):
                @block.gpsimd
                def _(g_):
                    g_.wait_ge(s_act.h, act_q)
                    emit_repacks(g_, u)
                    g_.wait_ge(s_act.h, act_q + 1)
                    s_rep.inc(g_.dma_start(q_aug[64:65, :], flat_src[:]), 16)
                    M[("flat", u)] = s_rep.n
                @block.tensor
                def _(t):
                    t.wait_ge(s_rep.h, M[("kdup", u)])
                    t.wait_ge(s_vec.h, vec_q)
                    for j in range(NCH):
                        if j >= 1:
                            t.wait_ge(s_vec.h, vec_q + 2 * j)
                        lqj = qg[0:128, j * TG + go: j * TG + go + 128]
                        for n in range(4):
                            t.matmul(psA[:, n * 512:(n + 1) * 512], lqj,
                                     kdA[0:128, n * 512:(n + 1) * 512],
                                     start=True, stop=True)
                        last = None
                        for n in range(4):
                            last = t.matmul(psB[:, n * 512:(n + 1) * 512], lqj,
                                     kdB[0:128, n * 512:(n + 1) * 512],
                                     start=True, stop=True)
                        s_mm.inc(last)
                @block.vector
                def _(v):
                    for j in range(NCH):
                        v.wait_ge(s_mm.h, mm_q + j + 1)
                        s_vec.inc(v.reduce_max(negmax[:, 2 * j:2 * j + 1], psA[:],
                                               axis=AX.X, negate=True))
                        s_vec.inc(v.reduce_max(negmax[:, 2 * j + 1:2 * j + 2], psB[:],
                                               axis=AX.X, negate=True))
                    M[("reds", u)] = s_vec.n
            with nc.psum_tensor(f"ps_nm_{u}", [16, 128], F32) as ps_nm:
                mmb2 = s_mm.n
                @block.tensor
                def _(t):
                    t.wait_ge(s_vec.h, M[("reds", u)])
                    s_mm.inc(t.transpose(ps_nm[:], negmax[:], ident[:]))
                @block.scalar
                def _(sc):
                    sc.wait_ge(s_mm.h, mmb2 + 1)
                    s_act.inc(sc.copy(flat_src[:], ps_nm[:]))
                    M[("flatevac", u)] = s_act.n

    zt = AE(nc.sbuf_tensor("zt", [128, 16], F32))
    zt2 = AE(nc.sbuf_tensor("zt2", [128, 16], F32))
    negmaxA = AE(nc.sbuf_tensor("negmaxA", [128, 16], F32))
    negmaxB = AE(nc.sbuf_tensor("negmaxB", [128, 16], F32))
    M = {}

    def emit_repacks(g_, u):
        g = u // UPG
        go = (u % UPG) * 128
        kdAv = kdA[:].rearrange("p (c g2 t) -> p c g2 t", g2=2, t=128)
        kdBv = kdB[:].rearrange("p (c g2 t) -> p c g2 t", g2=2, t=128)
        kav = k_aug[:].rearrange("p (c g2 t) -> p c g2 t", g2=2, t=128)
        qav = q_aug[:].rearrange("p (c g2 t) -> p c g2 t", g2=2, t=128)
        kgv_t = kg[0:64, :].rearrange("p (c t) -> p c t", t=TG)[:, :, go:go + 128]
        kgv_b = kg[64:128, :].rearrange("p (c t) -> p c t", t=TG)[:, :, go:go + 128]
        qgv_t = qg[0:64, :].rearrange("p (c t) -> p c t", t=TG)[:, :, go:go + 128]
        qgv_b = qg[64:128, :].rearrange("p (c t) -> p c t", t=TG)[:, :, go:go + 128]
        for dst, srcv in ((kdAv[0:64, :, 0, :], kgv_t), (kdAv[0:64, :, 1, :], kgv_b),
                          (kdBv[64:128, :, 0, :], kgv_t), (kdBv[64:128, :, 1, :], kgv_b)):
            s_rep.inc(g_.dma_start(dst, srcv), 16)
        M[("kdup", u)] = s_rep.n
        for dst, srcv in ((kav[0:64, :, 0, :], kgv_t), (kav[0:64, :, 1, :], kgv_b),
                          (qav[0:64, :, 0, :], qgv_t), (qav[0:64, :, 1, :], qgv_b)):
            s_rep.inc(g_.dma_start(dst, srcv), 16)


    pv_all = AE(nc.sbuf_tensor("pv_all", [65, UPG * S], BF16))
    ctxn_g = AE(nc.sbuf_tensor("ctxn_g", [128, NCH * TG], BF16))

    def attz_block(block, g, uo):
        u = g * UPG + uo
        nxt = u + 1 if uo < UPG - 1 else None
        ngo = (uo + 1) * 128
        if True:
            with (nc.psum_tensor(f"zpv_{u}", [128, S], F32) as ps_PV,
                  nc.psum_tensor(f"zst_{u}", [128, 1024], F32) as ps_St,
                  nc.psum_tensor(f"zs_{u}", [128, 1024], F32) as ps_Sz):
                mm0, act0, vec0 = s_mm.n, s_act.n, s_vec.n
                per_f = 6 if nxt is not None else 4
                def mm_st(f, h):   return mm0 + per_f * f + (1 if h == 0 else (3 if nxt is not None else 2))
                def mm_sz(f, h):   return mm0 + per_f * f + (2 if h == 0 else 4)
                def mm_pv(f, h):   return mm0 + per_f * f + (per_f - 1 if h == 0 else per_f)
                def act_exp(f, h): return act0 + 2 * f + h + 1
                def vec_red(f, h): return vec0 + 2 * f + h + 1
                tE = mm0 + 16 * per_f
                if nxt is not None:
                    M[("nmtr", nxt)] = tE + 1
                    M[("reds", nxt)] = vec0 + 33
                M[("st15", u)] = mm_st(15, 1)
                if nxt is not None:
                    M[("szlast", u)] = mm_sz(15, 1)
                M[("pvdone", u)] = mm_pv(15, 1)
                M[("expdone", u)] = act0 + 32
                M[("pvevac", u)] = act0 + 33
                if nxt is not None:
                    M[("flatevac", nxt)] = act0 + 34

                @block.gpsimd
                def _(g_):
                    if nxt is not None:
                        if uo >= 1:
                            g_.wait_ge(s_mm.h, M[("szlast", u - 1)])
                        kdAv = kdA[:].rearrange("p (c g2 t) -> p c g2 t", g2=2, t=128)
                        kdBv = kdB[:].rearrange("p (c g2 t) -> p c g2 t", g2=2, t=128)
                        kgv_t = kg[0:64, :].rearrange("p (c t) -> p c t", t=TG)[:, :, ngo:ngo + 128]
                        kgv_b = kg[64:128, :].rearrange("p (c t) -> p c t", t=TG)[:, :, ngo:ngo + 128]
                        for dst, srcv in ((kdAv[0:64, :, 0, :], kgv_t), (kdAv[0:64, :, 1, :], kgv_b),
                                          (kdBv[64:128, :, 0, :], kgv_t), (kdBv[64:128, :, 1, :], kgv_b)):
                            s_rep.inc(g_.dma_start(dst, srcv), 16)
                        M[("kdup", nxt)] = s_rep.n
                        g_.wait_ge(s_mm.h, M[("st15", u)])
                        kav = k_aug[:].rearrange("p (c g2 t) -> p c g2 t", g2=2, t=128)
                        qav = q_aug[:].rearrange("p (c g2 t) -> p c g2 t", g2=2, t=128)
                        qgv_t = qg[0:64, :].rearrange("p (c t) -> p c t", t=TG)[:, :, ngo:ngo + 128]
                        qgv_b = qg[64:128, :].rearrange("p (c t) -> p c t", t=TG)[:, :, ngo:ngo + 128]
                        for dst, srcv in ((kav[0:64, :, 0, :], kgv_t), (kav[0:64, :, 1, :], kgv_b),
                                          (qav[0:64, :, 0, :], qgv_t), (qav[0:64, :, 1, :], qgv_b)):
                            s_rep.inc(g_.dma_start(dst, srcv), 16)
                        g_.wait_ge(s_act.h, M[("flatevac", nxt)])
                        s_rep.inc(g_.dma_start(q_aug[64:65, :], flat_src[:]), 16)
                        M[("flat", nxt)] = s_rep.n

                @block.tensor
                def _(t):
                    t.wait_ge(s_rep.h, M[("flat", u)])
                    t.wait_ge(s_act.h, M.get(("expdone", u - 1), 0))
                    for f in range(16):
                        for h in range(2):
                            if f >= 1 or h == 1:
                                t.wait_ge(s_act.h, act_exp(f - 1 + h, 1 - h))
                            last = None
                            for n in range(2):
                                rq = h * 1024 + n * 512
                                last = t.matmul(ps_St[:, n * 512:(n + 1) * 512],
                                         k_aug[0:128, f * 128:(f + 1) * 128],
                                         q_aug[0:128, rq:rq + 512],
                                         start=True, stop=True)
                            s_mm.inc(last)
                            if nxt is not None:
                                if f >= 1 or h == 1:
                                    t.wait_ge(s_vec.h, vec_red(f - 1 + h, 1 - h))
                                if f == 0 and h == 0:
                                    t.wait_ge(s_rep.h, M[("kdup", nxt)])
                                    t.wait_ge(s_act.h, M[("flatevac", u)])
                                kd = kdA if f % 2 == 0 else kdB
                                lq = qg[0:128, (f // 2) * TG + ngo: (f // 2) * TG + ngo + 128]
                                last = None
                                for n in range(2):
                                    rp = h * 1024 + n * 512
                                    last = t.matmul(ps_Sz[:, n * 512:(n + 1) * 512], lq,
                                             kd[0:128, rp:rp + 512],
                                             start=True, stop=True)
                                s_mm.inc(last)
                        for h in range(2):
                            t.wait_ge(s_act.h, act_exp(f, h))
                            if f == 0 and h == 0:
                                t.wait_ge(s_act.h, M.get(("pvevac", u - 1), 0))
                            last = None
                            for n in range(2):
                                rq = h * 1024 + n * 512
                                last = t.matmul(ps_PV[:, rq:rq + 512],
                                         vaug[:, uo * 1040 + f * 65: uo * 1040 + f * 65 + 128],
                                         Pt[:, rq:rq + 512],
                                         start=(f == 0), stop=(f == 15))
                            s_mm.inc(last)
                    if nxt is not None:
                        t.wait_ge(s_vec.h, M[("reds", nxt)])
                        s_mm.inc(t.transpose(ps_Sz[0:16, 512:640], negmax[:], ident[:]))

                @block.scalar
                def _(sc):
                    for f in range(16):
                        for h in range(2):
                            sc.wait_ge(s_mm.h, mm_st(f, h))
                            if f == 0 and h == 0:
                                sc.wait_ge(s_mm.h, M.get(("pvdone", u - 1), 0))
                            if f >= 1:
                                sc.wait_ge(s_mm.h, mm_pv(f - 1, h))
                            s_act.inc(sc.activation(Pt[:, h * 1024:(h + 1) * 1024],
                                                    ps_St[:], AF.Exp))
                    sc.wait_ge(s_mm.h, M[("pvdone", u)])
                    s_act.inc(sc.activation(pv_all[:, uo * S:(uo + 1) * S],
                                            ps_PV[0:65, :], AF.Copy))
                    if nxt is not None:
                        sc.wait_ge(s_mm.h, M[("nmtr", nxt)])
                        s_act.inc(sc.copy(flat_src[:], ps_Sz[0:16, 512:640]))

                @block.vector
                def _(v):
                    if nxt is not None:
                        for f in range(16):
                            for h in range(2):
                                v.wait_ge(s_mm.h, mm_sz(f, h))
                                tgt = negmaxA if h == 0 else negmaxB
                                s_vec.inc(v.reduce_max(tgt[:, f:f + 1], ps_Sz[:],
                                                       axis=AX.X, negate=True))
                        v.tensor_copy(scr8[:], negmaxA[:, 0:8])
                        s_vec.inc(v.tensor_tensor(negmax[:], negmaxA[:], negmaxB[:],
                                                  ALU.min))
                        assert s_vec.n == M[("reds", nxt)]

    def attepi_block(block, g):
        if True:
            with (nc.psum_tensor(f"ebz_{g}", [64, S], F32) as ps_bz,
                  nc.psum_tensor(f"eln_{g}", [128, 512], F32) as ps_ln,
                  nc.psum_tensor(f"ef0_{g}", [128, 512], F32) as ps_f0,
                  nc.psum_tensor(f"ef1_{g}", [128, 512], F32) as ps_f1):
                ps_F = [ps_f0, ps_f1]
                mm0, act0, vec0, rep0 = s_mm.n, s_act.n, s_vec.n, s_rep.n
                A_ctxsq = [act0 + 2 * i + 1 for i in range(UPG)]
                A_rstd = [act0 + 2 * i + 2 for i in range(UPG)]
                T_bz = [mm0 + 3 * i + 1 for i in range(UPG)]
                T_ln = [mm0 + 3 * i + 2 for i in range(UPG)]
                T_b2 = [mm0 + 3 * i + 3 for i in range(UPG)]
                T_f = [mm0 + 3 * UPG + oc + 1 for oc in range(NCH)]
                V_rec = [vec0 + 6 * i + 1 for i in range(UPG)]
                V_pvm = [vec0 + 6 * i + 2 for i in range(UPG)]
                V_ctx2 = [vec0 + 6 * i + 3 for i in range(UPG)]
                V_var = [vec0 + 6 * i + 4 for i in range(UPG)]
                V_nmr = [vec0 + 6 * i + 5 for i in range(UPG)]
                V_ctxn = [vec0 + 6 * i + 6 for i in range(UPG)]
                V_fev = [vec0 + 6 * UPG + oc + 1 for oc in range(NCH)]
                R_z1 = [rep0 + 64 * i + 16 for i in range(UPG)]
                R_invz = [rep0 + 64 * i + 32 for i in range(UPG)]
                R_remap = [rep0 + 64 * i + 64 for i in range(UPG)]

                @block.scalar
                def _(sc):
                    for i in range(UPG):
                        sc.wait_ge(s_vec.h, V_ctx2[i])
                        s_act.inc(sc.activation(ctxsq[:], ctx2u[:], AF.Square))
                        sc.wait_ge(s_vec.h, V_var[i])
                        sc.activation(lnv_r[0:1, 0:128], var_r[0:1, 0:128], AF.Ln)
                        s_act.inc(sc.activation(rstd_bf[0:1, 0:128], lnv_r[0:1, 0:128],
                                                AF.Exp, bias=0.0, scale=-0.5))

                @block.tensor
                def _(t):
                    t.wait_ge(s_act.h, M[("expdone", g * UPG + UPG - 1)])
                    t.wait_ge(s_act.h, M[("pvevac", g * UPG + UPG - 1)])
                    for i in range(UPG):
                        t.wait_ge(s_rep.h, R_invz[i])
                        if i >= 1:
                            t.wait_ge(s_vec.h, V_pvm[i - 1])
                        last = None
                        for n in range(4):
                            last = t.matmul(ps_bz[:, n * 512:(n + 1) * 512],
                                     ones_row[0:1, 0:64],
                                     invZ_bf[0:1, n * 512:(n + 1) * 512],
                                     start=True, stop=True)
                        s_mm.inc(last)
                        t.wait_ge(s_vec.h, V_ctx2[i])
                        if i >= 1:
                            t.wait_ge(s_vec.h, V_ctxn[i - 1])
                        for c in range(NCH):
                            t.matmul(ps_ln[0:1, 0:128], ones_col[:],
                                     ctx2u[:, c * 128:(c + 1) * 128],
                                     start=(c == 0), stop=(c == NCH - 1))
                        t.wait_ge(s_act.h, A_ctxsq[i])
                        last = None
                        for c in range(NCH):
                            last = t.matmul(ps_ln[0:1, 128:256], ones_col[:],
                                     ctxsq[:, c * 128:(c + 1) * 128],
                                     start=(c == 0), stop=(c == NCH - 1))
                        s_mm.inc(last)
                        t.wait_ge(s_vec.h, V_nmr[i])
                        t.matmul(ps_ln[:, 256:384], ones_row[0:1, 0:128],
                                 rstd_bf[0:1, 0:128], start=True, stop=True)
                        s_mm.inc(t.matmul(ps_ln[:, 384:512], ones_row[0:1, 0:128],
                                 nmr_bf[0:1, 0:128], start=True, stop=True))
                    t.wait_ge(s_ld.h, LD_W)
                    for oc in range(NCH):
                        p = ps_F[oc % 2]
                        t.wait_ge(s_vec.h, V_ctxn[UPG - 1])
                        if oc >= 2:
                            t.wait_ge(s_vec.h, V_fev[oc - 2])
                        for dc in range(NCH):
                            t.matmul(p[:], wf_sb[:, dc * D + oc * 128: dc * D + (oc + 1) * 128],
                                     ctxn_g[:, dc * TG:(dc + 1) * TG],
                                     start=(dc == 0), stop=False)
                        s_mm.inc(t.matmul(p[:], bf_row[0:1, oc * 128:(oc + 1) * 128],
                                 ones_row[0:1, :], start=False, stop=True))

                @block.vector
                def _(v):
                    for i in range(UPG):
                        u = g * UPG + i
                        go_i = i * 128
                        v.wait_ge(s_rep.h, R_z1[i])
                        with nc.allow_low_precision(reason="1/Z softmax scale via bf16 round-trip, within tolerance"):
                            s_vec.inc(v.reciprocal(zt2[:], zt[:]))
                        v.wait_ge(s_mm.h, T_bz[i])
                        if i >= 1:
                            v.wait_ge(s_rep.h, R_remap[i - 1])
                        s_vec.inc(v.tensor_mul(pvctx[:], pv_all[0:64, i * S:(i + 1) * S],
                                               ps_bz[:]))
                        v.wait_ge(s_rep.h, R_remap[i])
                        h1v = h1g[:].rearrange("p (c t) -> p c t", t=TG)[:, :, go_i:go_i + 128]
                        c2v = ctx2u[:].rearrange("p (c t) -> p c t", t=128)
                        s_vec.inc(v.tensor_tensor(c2v, c2v, h1v, ALU.add))
                        v.wait_ge(s_mm.h, T_ln[i])
                        v.tensor_scalar(mean_r[0:1, 0:128], ps_ln[0:1, 0:128],
                                        INV_D, None, ALU.mult)
                        v.tensor_mul(msq_r[0:1, 0:128], mean_r[0:1, 0:128],
                                     mean_r[0:1, 0:128])
                        v.tensor_scalar(msq_r[0:1, 0:128], msq_r[0:1, 0:128],
                                        1.0, -EPS, ALU.mult, ALU.add)
                        s_vec.inc(v.scalar_tensor_tensor(var_r[0:1, 0:128],
                                                         ps_ln[0:1, 128:256], INV_D,
                                                         msq_r[0:1, 0:128],
                                                         ALU.mult, ALU.subtract))
                        v.wait_ge(s_act.h, A_rstd[i])
                        s_vec.inc(v.scalar_tensor_tensor(nmr_bf[0:1, 0:128],
                                                         mean_r[0:1, 0:128], -1.0,
                                                         rstd_bf[0:1, 0:128],
                                                         ALU.mult, ALU.mult))
                        v.wait_ge(s_mm.h, T_b2[i])
                        for c in range(NCH):
                            v.tensor_tensor(ctxn_g[:, c * TG + go_i: c * TG + go_i + 128],
                                            ctx2u[:, c * 128:(c + 1) * 128],
                                            ps_ln[:, 256:384], ALU.mult)
                            ins = v.tensor_tensor(ctxn_g[:, c * TG + go_i: c * TG + go_i + 128],
                                                  ctxn_g[:, c * TG + go_i: c * TG + go_i + 128],
                                                  ps_ln[:, 384:512], ALU.add)
                            if c == NCH - 1:
                                s_vec.inc(ins)
                    for oc in range(NCH):
                        v.wait_ge(s_mm.h, T_f[oc])
                        xs = x_sb[:, oc * S + g * TG: oc * S + (g + 1) * TG]
                        s_vec.inc(v.scalar_tensor_tensor(
                            xs, ps_F[oc % 2][:], ga_t[:, oc:oc + 1], xs,
                            ALU.mult, ALU.add))

                @block.gpsimd
                def _(g_):
                    for i in range(UPG):
                        u = g * UPG + i
                        if i >= 1:
                            g_.wait_ge(s_vec.h, V_rec[i - 1])
                        g_.wait_ge(s_act.h, M[("pvevac", u)])
                        s_rep.inc(g_.dma_start(zt[:], pv_all[64:65, i * S:(i + 1) * S]), 16)
                        g_.wait_ge(s_vec.h, V_rec[i])
                        if i >= 1:
                            g_.wait_ge(s_mm.h, T_bz[i - 1])
                        s_rep.inc(g_.dma_start(invZ_bf[:], zt2[:]), 16)
                        g_.wait_ge(s_vec.h, V_pvm[i])
                        if i >= 1:
                            g_.wait_ge(s_vec.h, V_ctxn[i - 1])
                        pv_e = pvctx[:].rearrange("p (c f2 t) -> p c f2 t", f2=2, t=128)
                        c2 = ctx2u[:].rearrange("p (c t) -> p c t", t=128)
                        s_rep.inc(g_.dma_start(c2[0:64, :, :], pv_e[:, :, 0, :]), 16)
                        s_rep.inc(g_.dma_start(c2[64:128, :, :], pv_e[:, :, 1, :]), 16)

                if g == 0 and any(nm in taps for nm in ("ctx2", "ctxn")):
                    @block.sync
                    def _(sy):
                        if "ctx2" in taps:
                            sy.wait_ge(s_vec.h, V_ctx2[0])
                            s_out.inc(sy.dma_start(tap_d["ctx2"], ctx2u[:]), 16)
                        if "ctxn" in taps:
                            sy.wait_ge(s_vec.h, V_ctxn[0])
                            s_out.inc(sy.dma_start(tap_d["ctxn"], ctxn_g[:, 0:1024]), 16)


    if stop_after not in ("mod", "battinit", "battnomemset"):
        for g in range(NG):
            ln_group(f"ln1_{g}", g, x_sb, h1g, True, ca1_t, sa_t, xsq)
            if stop_after == "ln1":
                break
            qkv_group(g)
            if stop_after == "qkv":
                break
            with nc.Block(f"attg_{g}") as ab:
                for uo in range(UPG):
                    attz_block(ab, g, uo)
                attepi_block(ab, g)
            if stop_after in ("unit0", "alpha0"):
                break

    att_scope.close()
    _skip_moe = stop_after in ("mod", "battinit", "battnomemset", "ln1", "qkv", "unit0", "alpha0", "att")

    if "x2" in taps:
        with nc.Block("tapx2") as block:
            @block.sync
            def _(sy):
                s_out.inc(sy.dma_start(tap_d["x2"], x_sb[:]), 16)

    moe_pre = contextlib.ExitStack()
    MP = moe_pre.enter_context
    h2 = E(nc.sbuf_tensor("h2", [128, NCH * S], BF16))
    coefT_bf = E(nc.sbuf_tensor("coefT_bf", [NE, S], BF16))
    sel_sb = E(nc.sbuf_tensor("sel_sb", [NE, NE * 128], BF16))
    xsq2 = MP(nc.sbuf_tensor("xsq2", [128, 1024], BF16))
    logits_s = MP(nc.sbuf_tensor("logits_s", [NE, S], F32))
    logits_tm = MP(nc.sbuf_tensor("logits_tm", [128, 128], F32))
    coef_tm = MP(nc.sbuf_tensor("coef_tm", [128, 128], F32))
    nm1 = MP(nc.sbuf_tensor("nm1", [128, 1], F32))
    z1 = MP(nc.sbuf_tensor("z1", [128, 1], F32))
    w1 = MP(nc.sbuf_tensor("w1", [128, 1], F32))
    scr_e = MP(nc.sbuf_tensor("scr_e", [128, 8], F32))
    scr_ge = MP(nc.sbuf_tensor("scr_ge", [128, 8], F32))
    scr_cu = MP(nc.sbuf_tensor("scr_cu", [128, 8], F32))
    scr_eq = MP(nc.sbuf_tensor("scr_eq", [128, 8], F32))
    scr_m = MP(nc.sbuf_tensor("scr_m", [128, 8], F32))

    if not _skip_moe:
     for g in range(NG):
        ln_group(f"ln3_{g}", g, x_sb, h2, False, cb1_t, sb_t, xsq2)

    if "h2t" in taps:
        with nc.Block("taph2") as block:
            @block.sync
            def _(sy):
                s_out.inc(sy.dma_start(tap_d["h2t"], h2[:]), 16)

    if not _skip_moe:
     with nc.Block("router") as block:
        with (nc.psum_tensor("ps_lg", [NE, S], F32) as ps_lg,
              nc.psum_tensor("ps_ltm", [128, 128], F32) as ps_ltm):
            mmb, actb = s_mm.n, s_act.n
            @block.tensor
            def _(t):
                t.wait_ge(s_ld.h, LD_SMALL)
                for n in range(4):
                    for dc in range(NCH):
                        t.matmul(ps_lg[:, n * 512:(n + 1) * 512],
                                 rwT_sb[:, dc * NE:(dc + 1) * NE],
                                 h2[:, dc * S + n * 512: dc * S + (n + 1) * 512],
                                 start=(dc == 0), stop=False)
                    last = t.matmul(ps_lg[:, n * 512:(n + 1) * 512], rb_row[:],
                             ones_row[0:1, :], start=False, stop=True)
                s_mm.inc(last)
                t.wait_ge(s_act.h, actb + 1)
                for tch in range(16):
                    last = t.transpose(ps_ltm[:, tch * 8:(tch + 1) * 8],
                                logits_s[0:NE, tch * 128:(tch + 1) * 128],
                                ident[0:NE, 0:NE])
                s_mm.inc(last)
            @block.scalar
            def _(sc):
                sc.wait_ge(s_mm.h, mmb + 1)
                s_act.inc(sc.activation(logits_s[:], ps_lg[:], AF.Copy))
                sc.wait_ge(s_mm.h, mmb + 2)
                s_act.inc(sc.copy(logits_tm[:], ps_ltm[:]))
            if "ltm" in taps:
                @block.sync
                def _(sy):
                    sy.wait_ge(s_act.h, s_act.n)
                    s_out.inc(sy.dma_start(tap_d["ltm"], logits_tm[:]), 16)

    if not _skip_moe:
     with nc.Block("coef") as block:
        vecb, actb = s_vec.n, s_act.n
        @block.vector
        def _(v):
            for tch in range(16):
                sl = logits_tm[:, tch * 8:(tch + 1) * 8]
                if tch >= 1:
                    v.wait_ge(s_act.h, actb + tch)
                s_vec.inc(v.reduce_max(nm1[:], sl, axis=AX.X, negate=True))
                v.wait_ge(s_act.h, actb + tch + 1)
                s_vec.inc(v.reciprocal(w1[:], z1[:]))
                v.tensor_scalar(scr_ge[:], sl, nm1[:], 0.0, ALU.add, ALU.is_ge)
                v.tensor_tensor_scan(scr_cu[:], scr_ge[:], scr_ge[:], 0.0,
                                     ALU.add, ALU.bypass)
                v.tensor_scalar(scr_eq[:], scr_cu[:], 1.0, None, ALU.is_equal)
                v.tensor_mul(scr_m[:], scr_ge[:], scr_eq[:])
                s_vec.inc(v.tensor_scalar(coef_tm[:, tch * 8:(tch + 1) * 8],
                                          scr_m[:], w1[:], None, ALU.mult))
        @block.scalar
        def _(sc):
            for tch in range(16):
                sc.wait_ge(s_vec.h, vecb + 3 * tch + 1)
                if tch >= 1:
                    sc.wait_ge(s_vec.h, vecb + 3 * (tch - 1) + 2)
                s_act.inc(sc.activation(scr_e[:], logits_tm[:, tch * 8:(tch + 1) * 8],
                                        AF.Exp, bias=nm1[:], scale=1.0,
                                        accum_out=z1[:]))
        if "coef" in taps:
            @block.sync
            def _(sy):
                sy.wait_ge(s_vec.h, s_vec.n)
                s_out.inc(sy.dma_start(tap_d["coef"], coef_tm[:]), 16)

    if not _skip_moe:
     with nc.Block("coefT") as block:
        with nc.psum_tensor("ps_cT", [NE, S], F32) as ps_cT:
            mmb = s_mm.n
            @block.tensor
            def _(t):
                last = None
                for tch in range(16):
                    last = t.transpose(ps_cT[0:NE, tch * 128:(tch + 1) * 128],
                                coef_tm[:, tch * 8:(tch + 1) * 8], ident[:])
                s_mm.inc(last)
            @block.scalar
            def _(sc):
                sc.wait_ge(s_mm.h, mmb + 1)
                s_act.inc(sc.activation(coefT_bf[:], ps_cT[:], AF.Copy))
        @block.sync
        def _(sy):
            s_ld.inc(sy.dma_start(sel_sb[:], sel_d.ap()), 16)
    moe_pre.close()

    moe_scope = contextlib.ExitStack()
    ME = moe_scope.enter_context
    Be_tc = ME(nc.sbuf_tensor("Be_tc", [128, NE * 512], BF16))
    ew_sb = [ME(nc.sbuf_tensor(f"ew_sb{i}", [128, NCH * D], BF16)) for i in range(2)]
    h2s = [ME(nc.sbuf_tensor(f"h2s{i}", [128, NCH * 512], BF16)) for i in range(2)]
    out_stage = ME(nc.sbuf_tensor("out_stage", [128, 4096], F32))

    if not _skip_moe:
     with nc.Block("experts") as block:
        with nc.psum_tensor("ps_moe", [128, 4096], F32) as ps_moe:
            mmb, vecb, actb, ewb, outb = s_mm.n, s_vec.n, s_act.n, s_ew.n, s_out.n
            @block.gpsimd
            def _(g):
                for tc in range(4):
                    for e in range(NE):
                        i = tc * NE + e
                        if i >= 2:
                            i2 = i - 2
                            g.wait_ge(s_mm.h, mmb + (i2 // NE) * 17 + 8 + (i2 % NE) + 1)
                        s_ew.inc(g.dma_start(ew_sb[i % 2][:], ew_d[e]), 16)
            @block.sync
            def _(sy):
                for tc in range(4):
                    for oc in range(NCH):
                        sy.wait_ge(s_vec.h, vecb + tc * 24 + 16 + oc + 1)
                        s_out.inc(sy.dma_start(
                            out_d[oc * 128:(oc + 1) * 128, tc * 512:(tc + 1) * 512],
                            out_stage[:, oc * 512:(oc + 1) * 512]), 16)
            @block.tensor
            def _(t):
                t.wait_ge(s_ld.h, s_ld.n)
                for tc in range(4):
                    for e in range(NE):
                        slot = (e % 2) * 512
                        if e >= 2:
                            t.wait_ge(s_vec.h, vecb + tc * 24 + e - 1)
                        if tc >= 1 and e < 2:
                            t.wait_ge(s_vec.h, vecb + (tc - 1) * 24 + 24)
                        s_mm.inc(t.matmul(ps_moe[:, slot:slot + 512],
                                 sel_sb[0:NE, e * 128:(e + 1) * 128],
                                 coefT_bf[0:NE, tc * 512:(tc + 1) * 512],
                                 start=True, stop=True))
                    for e in range(NE):
                        t.wait_ge(s_ew.h, ewb + (tc * NE + e + 1) * 16)
                        t.wait_ge(s_vec.h, vecb + tc * 24 + 8 + e + 1)
                        buf = ew_sb[(tc * NE + e) % 2]
                        hbuf = h2s[e % 2]
                        last = None
                        for oc in range(NCH):
                            for dc in range(NCH):
                                last = t.matmul(ps_moe[:, oc * 512:(oc + 1) * 512],
                                         buf[:, dc * D + oc * 128: dc * D + (oc + 1) * 128],
                                         hbuf[:, dc * 512:(dc + 1) * 512],
                                         start=(e == 0 and dc == 0), stop=False)
                        s_mm.inc(last)
                    last = None
                    for oc in range(NCH):
                        last = t.matmul(ps_moe[:, oc * 512:(oc + 1) * 512],
                                 eb_sb[0:NE, oc * 128:(oc + 1) * 128],
                                 coefT_bf[0:NE, tc * 512:(tc + 1) * 512],
                                 start=False, stop=True)
                    s_mm.inc(last)
            @block.vector
            def _(v):
                for tc in range(4):
                    for e in range(NE):
                        slot = (e % 2) * 512
                        v.wait_ge(s_mm.h, mmb + tc * 17 + e + 1)
                        s_vec.inc(v.tensor_copy(Be_tc[:, e * 512:(e + 1) * 512],
                                                ps_moe[:, slot:slot + 512]))
                    for e in range(NE):
                        hbuf = h2s[e % 2]
                        if e >= 2:
                            v.wait_ge(s_mm.h, mmb + tc * 17 + 8 + e - 1)
                        for dc in range(NCH):
                            ins = v.tensor_tensor(
                                hbuf[:, dc * 512:(dc + 1) * 512],
                                h2[:, dc * S + tc * 512: dc * S + (tc + 1) * 512],
                                Be_tc[:, e * 512:(e + 1) * 512], ALU.mult)
                            if dc == NCH - 1:
                                s_vec.inc(ins)
                    v.wait_ge(s_mm.h, mmb + tc * 17 + 17)
                    if tc >= 1:
                        v.wait_ge(s_out.h, outb + tc * NCH * 16)
                    for oc in range(NCH):
                        xs = x_sb[:, oc * S + tc * 512: oc * S + (tc + 1) * 512]
                        s_vec.inc(v.scalar_tensor_tensor(
                            out_stage[:, oc * 512:(oc + 1) * 512],
                            ps_moe[:, oc * 512:(oc + 1) * 512],
                            gb_t[:, oc:oc + 1], xs, ALU.mult, ALU.add))
    moe_scope.close()

    with nc.Block("fin") as block:
        @block.sync
        def _(sy):
            sy.wait_ge(s_out.h, s_out.n)

    es.close()
    nc.finalize()
    return nc, sorted(tap_d.keys())


_cache = {}

def _get_program(taps=()):
    key = tuple(sorted(taps))
    if key not in _cache:
        _cache[key] = build_program(taps=key)
    return _cache[key]


def make_in_maps(inputs):
    f32 = np.float32
    def cm(w):
        w = np.asarray(w, f32)
        return np.ascontiguousarray(w.reshape(NCH, 128, w.shape[1]).transpose(1, 0, 2)
                                    .reshape(128, NCH * w.shape[1]))
    x = np.asarray(inputs["L0_attn_x"], f32)
    C = np.asarray(inputs["L0_c"], f32)
    wmod0 = np.asarray(inputs["L0_wmod"], f32).reshape(D, 6, D).transpose(1, 0, 2)
    wmod = np.ascontiguousarray(
        wmod0.reshape(6, NCH, 128, D).transpose(0, 2, 1, 3).reshape(6, 128, NCH * D))
    ew0 = np.asarray(inputs["L0_ffn_expert_w"], f32).transpose(0, 2, 1)
    ew = np.ascontiguousarray(
        ew0.reshape(NE, NCH, 128, D).transpose(0, 2, 1, 3).reshape(NE, 128, NCH * D))
    shared = {
        "wq": cm(inputs["L0_attn_Wq"]),
        "wk": cm(inputs["L0_attn_Wk"]),
        "wv": cm(inputs["L0_attn_Wv"]),
        "wf": cm(inputs["L0_attn_Wf"]),
        "wmod": wmod,
        "bqr": np.ascontiguousarray(np.asarray(inputs["L0_attn_bq"], f32).reshape(1, D)),
        "bkr": np.ascontiguousarray(np.asarray(inputs["L0_attn_bk"], f32).reshape(1, D)),
        "bvr": np.ascontiguousarray(np.asarray(inputs["L0_attn_bv"], f32).reshape(1, D)),
        "bfr": np.ascontiguousarray(np.asarray(inputs["L0_attn_bf"], f32).reshape(1, D)),
        "rwT": np.ascontiguousarray(np.asarray(inputs["L0_ffn_router_w"], f32).T),
        "rbr": np.ascontiguousarray(np.asarray(inputs["L0_ffn_router_b"], f32).reshape(1, NE)),
        "ew": ew,
        "eb": np.ascontiguousarray(np.asarray(inputs["L0_ffn_expert_b"], f32)),
    }
    in_maps = []
    for b in range(8):
        order = [b] + [i for i in range(8) if i != b]
        m = dict(shared)
        xb = x[b].T
        m["x"] = np.ascontiguousarray(xb.reshape(NCH, 128, S).transpose(1, 0, 2)
                                      .reshape(128, NCH * S))
        m["cT"] = np.ascontiguousarray(C[order].T)
        in_maps.append(m)
    return in_maps


def kernel(**inputs):
    nc, _ = _get_program()
    in_maps = make_in_maps(inputs)
    res = run_bass_kernel_spmd(nc, in_maps, core_ids=list(range(8)))
    out = np.stack([np.asarray(r["out"]).T for r in res.results], axis=0)
    return out.astype(np.float32)


# version 15
# speedup vs baseline: 1.0210x; 1.0002x over previous
import numpy as np
import ml_dtypes
import contextlib
import concourse.bass as bass
import concourse.mybir as mybir
from concourse.bass_utils import run_bass_kernel_spmd
import concourse.bass_utils as _bu

if not getattr(_bu, "_ldw_opt_patch", False):
    _bu._ldw_opt_patch = True
    _orig_run_command = _bu.run_command

    def _run_command_ldw(argv, **kwargs):
        argv = ["--enable-ldw-opt=true" if a == "--enable-ldw-opt=false" else a
                for a in argv]
        return _orig_run_command(argv, **kwargs)

    _bu.run_command = _run_command_ldw


F32 = mybir.dt.float32
F32R = mybir.dt.float32r
BF16 = mybir.dt.bfloat16
AF = mybir.ActivationFunctionType
ALU = mybir.AluOpType
AX = mybir.AxisListType

D = 1024
S = 2048
NCH = 8
NU = 16
NG = 4
UPG = 4
TG = 512
NE = 8
EPS = 1e-5
INV_D = 1.0 / D


def build_program(taps=(), stop_after=None):
    taps = set(taps)
    _stop = [stop_after]
    nc = bass.Bass("TRN2", target_bir_lowering=False, debug=False)

    x_d = nc.dram_tensor("x", [128, NCH * S], F32, kind="ExternalInput").ap()
    wq_d = nc.dram_tensor("wq", [128, NCH * D], F32, kind="ExternalInput").ap()
    wk_d = nc.dram_tensor("wk", [128, NCH * D], F32, kind="ExternalInput").ap()
    wv_d = nc.dram_tensor("wv", [128, NCH * D], F32, kind="ExternalInput").ap()
    wf_d = nc.dram_tensor("wf", [128, NCH * D], F32, kind="ExternalInput").ap()
    wmod_d = nc.dram_tensor("wmod", [6, 128, NCH * D], F32, kind="ExternalInput").ap()
    cT_d = nc.dram_tensor("cT", [D, NE], F32, kind="ExternalInput").ap()
    bq_d = nc.dram_tensor("bqr", [1, D], F32, kind="ExternalInput").ap()
    bk_d = nc.dram_tensor("bkr", [1, D], F32, kind="ExternalInput").ap()
    bv_d = nc.dram_tensor("bvr", [1, D], F32, kind="ExternalInput").ap()
    bf_d = nc.dram_tensor("bfr", [1, D], F32, kind="ExternalInput").ap()
    rwT_d = nc.dram_tensor("rwT", [D, NE], F32, kind="ExternalInput").ap()
    rb_d = nc.dram_tensor("rbr", [1, NE], F32, kind="ExternalInput").ap()
    ew_d = nc.dram_tensor("ew", [NE, 128, NCH * D], F32, kind="ExternalInput").ap()
    eb_d = nc.dram_tensor("eb", [NE, D], F32, kind="ExternalInput").ap()
    out_d = nc.dram_tensor("out", [D, S], F32, kind="ExternalOutput").ap()

    id_d = nc.inline_tensor(np.eye(128, dtype=np.float32), "identf")
    sel_np = np.zeros((NE, NE * 128), dtype=ml_dtypes.bfloat16)
    for _e in range(NE):
        sel_np[_e, _e * 128:(_e + 1) * 128] = 1.0
    sel_d = nc.inline_tensor(sel_np, "sel_const")

    tap_d = {}
    def deftap(name, shape, dtype=F32):
        if name in taps:
            tap_d[name] = nc.dram_tensor("tap_" + name, shape, dtype,
                                         kind="ExternalOutput").ap()
    deftap("mod", [128, 48]); deftap("h1", [128, NCH * TG], BF16)
    deftap("qg", [128, NCH * TG], BF16); deftap("kg", [128, NCH * TG], BF16)
    deftap("vaug", [128, UPG * 16 * 65], BF16)
    deftap("negmax", [128, 16]); deftap("kdup", [128, S], BF16)
    deftap("kaug", [128, S], BF16); deftap("qaug", [128, S], BF16)
    deftap("pt", [128, S], BF16)
    deftap("ctx2", [128, 1024], BF16); deftap("ctxn", [128, 1024], BF16)
    deftap("x2", [128, NCH * S], BF16); deftap("h2t", [128, NCH * S], BF16)
    deftap("ltm", [128, 128]); deftap("coef", [128, 128])

    es = contextlib.ExitStack()
    E = es.enter_context

    x_sb = E(nc.sbuf_tensor("x_sb", [128, NCH * S], BF16))
    mod_raw = E(nc.sbuf_tensor("mod_raw", [128, 48], F32))
    sa_t = E(nc.sbuf_tensor("sa_t", [128, 8], F32))
    ca1_t = E(nc.sbuf_tensor("ca1_t", [128, 8], F32))
    ga_t = E(nc.sbuf_tensor("ga_t", [128, 8], F32))
    sb_t = E(nc.sbuf_tensor("sb_t", [128, 8], F32))
    cb1_t = E(nc.sbuf_tensor("cb1_t", [128, 8], F32))
    gb_t = E(nc.sbuf_tensor("gb_t", [128, 8], F32))
    scr8 = E(nc.sbuf_tensor("scr8x", [128, 8], F32))
    scr8b = E(nc.sbuf_tensor("scr8b", [128, 8], F32))
    scr8u = E(nc.sbuf_tensor("scr8u", [128, 8], mybir.dt.uint8))
    rb_row = E(nc.sbuf_tensor("rb_row", [1, NE], BF16))
    rwT_sb = E(nc.sbuf_tensor("rwT_sb", [128, NCH * NE], BF16))
    eb_sb = E(nc.sbuf_tensor("eb_sb", [NE, D], BF16))
    ident = E(nc.sbuf_tensor("ident", [128, 128], F32))
    ones_col = E(nc.sbuf_tensor("ones_col", [128, 1], BF16))
    ones_row = E(nc.sbuf_tensor("ones_row", [1, 512], BF16))

    sems = {}
    for name in ("ld", "rep", "mm", "vec", "act", "out", "ew"):
        sems[name] = [E(nc.semaphore("s_" + name)), 0]

    class SemRef:
        __slots__ = ("key",)
        def __init__(self, key): self.key = key
        @property
        def h(self): return sems[self.key][0]
        @property
        def n(self): return sems[self.key][1]
        def inc(self, instr, k=1):
            instr.then_inc(self.h, k); sems[self.key][1] += k
            return sems[self.key][1]
    s_ld, s_rep, s_mm, s_vec, s_act, s_out, s_ew = (SemRef(k) for k in
        ("ld", "rep", "mm", "vec", "act", "out", "ew"))

    def rcp(ap):
        return ap.rearrange("(c p) n -> p c n", p=128)

    L = E
    mean_r = L(nc.sbuf_tensor("mean_r", [1, 512], F32))
    msq_r = L(nc.sbuf_tensor("msq_r", [1, 512], F32))
    var_r = L(nc.sbuf_tensor("var_r", [1, 512], F32))
    lnv_r = L(nc.sbuf_tensor("lnv_r", [1, 512], F32))
    rstd_bf = L(nc.sbuf_tensor("rstd_bf", [1, 512], BF16))
    nmr_bf = L(nc.sbuf_tensor("nmr_bf", [1, 512], BF16))
    cT_sb = L(nc.sbuf_tensor("cT_sb", [128, NCH * NE], F32))
    siluC = L(nc.sbuf_tensor("siluC", [128, NCH * NE], BF16))

    with nc.Block("b0") as block:
        @block.vector
        def _(v):
            v.memset(ones_col[:], 1.0)
            v.memset(ones_row[:], 1.0)
        @block.sync
        def _(sy):
            s_ld.inc(sy.dma_start(ident[:], id_d.ap()), 16)
            s_ld.inc(sy.dma_start(cT_sb[:], rcp(cT_d)), 16)
        @block.gpsimd
        def _(g):
            s_ld.inc(g.dma_start(x_sb[:], x_d), 16)
            s_ld.inc(g.dma_start(rb_row[:], rb_d), 16)
            s_ld.inc(g.dma_start(eb_sb[:], eb_d), 16)
            s_ld.inc(g.dma_start(rwT_sb[:], rcp(rwT_d)), 16)
    LD_SMALL = s_ld.n

    with contextlib.ExitStack() as wsc:
        wmod_sb = [wsc.enter_context(nc.sbuf_tensor(f"wmod{i}", [128, NCH * D], BF16))
                   for i in range(2)]
        modp_sb = wsc.enter_context(nc.sbuf_tensor("modp_sb", [NE, 6 * D], F32))
        with nc.Block("b1") as block:
            with (nc.psum_tensor("ps_mod0", [NE, D], F32) as ps_mod0,
                  nc.psum_tensor("ps_mod1", [NE, D], F32) as ps_mod1,
                  nc.psum_tensor("ps_tr", [128, 384], F32) as ps_tr):
                ps_mod = [ps_mod0, ps_mod1]
                mmb, actb = s_mm.n, s_act.n
                wmod_ld = []
                @block.gpsimd
                def _(g):
                    for p in range(6):
                        if p >= 2:
                            g.wait_ge(s_mm.h, mmb + 2 * (p - 2) + 1)
                        s_ld.inc(g.dma_start(wmod_sb[p % 2][:], wmod_d[p]), 16)
                        wmod_ld.append(s_ld.n)
                @block.scalar
                def _(sc):
                    sc.wait_ge(s_ld.h, LD_SMALL)
                    s_act.inc(sc.activation(siluC[:], cT_sb[:], AF.Silu))
                    for p in range(6):
                        sc.wait_ge(s_mm.h, mmb + 2 * p + 1)
                        s_act.inc(sc.activation(modp_sb[:, p * D:(p + 1) * D],
                                                ps_mod[p % 2][:], AF.Copy))
                    sc.wait_ge(s_mm.h, mmb + 12)
                    src = ps_tr[:].rearrange("p (j k) -> p j k", k=8)[:, :, 0:1]
                    s_act.inc(sc.copy(mod_raw[:], src))
                @block.tensor
                def _(t):
                    t.wait_ge(s_act.h, actb + 1)
                    for p in range(6):
                        t.wait_ge(s_ld.h, wmod_ld[p])
                        if p >= 2:
                            t.wait_ge(s_act.h, actb + 2 + (p - 2))
                        w = wmod_sb[p % 2]
                        last = None
                        for dc in range(NCH):
                            for h in range(2):
                                last = t.matmul(ps_mod[p % 2][:, h * 512:(h + 1) * 512],
                                         siluC[:, dc * NE:(dc + 1) * NE],
                                         w[:, dc * D + h * 512: dc * D + (h + 1) * 512],
                                         start=(dc == 0), stop=(dc == NCH - 1))
                        s_mm.inc(last)
                        t.wait_ge(s_act.h, actb + 2 + p)
                        for c in range(NCH):
                            last = t.transpose(ps_tr[:, (p * 8 + c) * 8:(p * 8 + c) * 8 + 8],
                                        modp_sb[0:NE, p * D + c * 128: p * D + (c + 1) * 128],
                                        ident[0:NE, 0:NE])
                        s_mm.inc(last)
                @block.vector
                def _(v):
                    v.wait_ge(s_act.h, actb + 8)
                    v.tensor_copy(sa_t[:], mod_raw[:, 0:8])
                    v.tensor_scalar(ca1_t[:], mod_raw[:, 8:16], 1.0, None, ALU.add)
                    v.tensor_copy(sb_t[:], mod_raw[:, 24:32])
                    v.tensor_scalar(cb1_t[:], mod_raw[:, 32:40], 1.0, None, ALU.add)
                    for gsrc, gdst in ((mod_raw[:, 16:24], ga_t), (mod_raw[:, 40:48], gb_t)):
                        v.tensor_scalar(scr8[:], gsrc, 0.25, None, ALU.max)
                        v.tensor_scalar(scr8b[:], gsrc, -0.25, None, ALU.min)
                        v.tensor_scalar(scr8u[:], gsrc, 0.0, None, ALU.is_ge)
                        s_vec.inc(v.select(gdst[:], scr8u[:], scr8[:], scr8b[:]))
                if "mod" in taps:
                    @block.sync
                    def _(sy):
                        sy.wait_ge(s_vec.h, s_vec.n)
                        s_out.inc(sy.dma_start(tap_d["mod"], mod_raw[:]), 16)
    MOD_DONE = s_vec.n

    def ln_group(name, g, src_sb, dst_sb, dst_is_group, c1_t, s_t, xsq_buf,
                 block=None):
        t0 = g * TG
        with (nc.Block(name) if block is None
              else contextlib.nullcontext(block)) as block:
            with (nc.psum_tensor(name + "_stat", [1, 1024], F32) as ps_stat,
                  nc.psum_tensor(name + "_bc", [128, 1024], F32) as ps_bc):
                mmb, actb, vecb = s_mm.n, s_act.n, s_vec.n
                @block.scalar
                def _(sc):
                    sc.wait_ge(s_ld.h, LD_SMALL)
                    for c in range(NCH):
                        if c >= 2:
                            sc.wait_ge(s_mm.h, mmb + (c - 1))
                        s_act.inc(sc.activation(xsq_buf[:, (c % 2) * 512:(c % 2 + 1) * 512],
                                                src_sb[:, c * S + t0: c * S + t0 + TG],
                                                AF.Square))
                    sc.wait_ge(s_vec.h, vecb + 2)
                    s_act.inc(sc.activation(lnv_r[:], var_r[:], AF.Ln))
                    s_act.inc(sc.activation(rstd_bf[:], lnv_r[:], AF.Exp,
                                            bias=0.0, scale=-0.5))
                @block.tensor
                def _(t):
                    t.wait_ge(s_ld.h, LD_SMALL)
                    for c in range(NCH):
                        t.matmul(ps_stat[0:1, 0:512], ones_col[:],
                                 src_sb[:, c * S + t0: c * S + t0 + TG],
                                 start=(c == 0), stop=(c == NCH - 1))
                    for c in range(NCH):
                        t.wait_ge(s_act.h, actb + c + 1)
                        s_mm.inc(t.matmul(ps_stat[0:1, 512:1024], ones_col[:],
                                 xsq_buf[:, (c % 2) * 512:(c % 2 + 1) * 512],
                                 start=(c == 0), stop=(c == NCH - 1)))
                    t.wait_ge(s_vec.h, vecb + 3)
                    t.matmul(ps_bc[:, 0:512], ones_row[0:1, 0:128], rstd_bf[:],
                             start=True, stop=True)
                    s_mm.inc(t.matmul(ps_bc[:, 512:1024], ones_row[0:1, 0:128], nmr_bf[:],
                             start=True, stop=True))
                @block.vector
                def _(v):
                    v.wait_ge(s_mm.h, mmb + 8)
                    v.tensor_scalar(mean_r[:], ps_stat[0:1, 0:512], INV_D, None, ALU.mult)
                    v.tensor_mul(msq_r[:], mean_r[:], mean_r[:])
                    s_vec.inc(v.tensor_scalar(msq_r[:], msq_r[:], 1.0, -EPS,
                                              ALU.mult, ALU.add))
                    s_vec.inc(v.scalar_tensor_tensor(var_r[:], ps_stat[0:1, 512:1024],
                                                     INV_D, msq_r[:],
                                                     ALU.mult, ALU.subtract))
                    v.wait_ge(s_act.h, actb + 10)
                    s_vec.inc(v.scalar_tensor_tensor(nmr_bf[:], mean_r[:], -1.0,
                                                     rstd_bf[:], ALU.mult, ALU.mult))
                    v.wait_ge(s_mm.h, mmb + 9)
                    v.wait_ge(s_vec.h, MOD_DONE)
                    for c in range(NCH):
                        if dst_is_group:
                            dslc = dst_sb[:, c * TG: (c + 1) * TG]
                        else:
                            dslc = dst_sb[:, c * S + t0: c * S + t0 + TG]
                        v.tensor_tensor(dslc, src_sb[:, c * S + t0: c * S + t0 + TG],
                                        ps_bc[:, 0:512], ALU.mult)
                        v.tensor_tensor(dslc, dslc, ps_bc[:, 512:1024], ALU.add)
                        s_vec.inc(v.tensor_scalar(dslc, dslc, c1_t[:, c:c + 1],
                                                  s_t[:, c:c + 1], ALU.mult, ALU.add))

    att_scope = contextlib.ExitStack()
    AE = att_scope.enter_context
    wq_sb = AE(nc.sbuf_tensor("wq_sb", [128, NCH * D], BF16))
    wk_sb = AE(nc.sbuf_tensor("wk_sb", [128, NCH * D], BF16))
    wv_sb = AE(nc.sbuf_tensor("wv_sb", [128, NCH * D], BF16))
    wf_sb = AE(nc.sbuf_tensor("wf_sb", [128, NCH * D], BF16))
    bq_row = AE(nc.sbuf_tensor("bq_row", [1, D], BF16))
    bk_row = AE(nc.sbuf_tensor("bk_row", [1, D], BF16))
    bv_row = AE(nc.sbuf_tensor("bv_row", [1, D], BF16))
    bf_row = AE(nc.sbuf_tensor("bf_row", [1, D], BF16))
    h1g = AE(nc.sbuf_tensor("h1g", [128, NCH * TG], BF16))
    qg = AE(nc.sbuf_tensor("qg", [128, NCH * TG], BF16))
    kg = AE(nc.sbuf_tensor("kg", [128, NCH * TG], BF16))
    vaug = AE(nc.sbuf_tensor("vaug", [128, UPG * 16 * 65 + 64], BF16))
    xsq = AE(nc.sbuf_tensor("xsq", [128, 1024], BF16))
    kdA = AE(nc.sbuf_tensor("kdA", [128, S], BF16))
    kdB = AE(nc.sbuf_tensor("kdB", [128, S], BF16))
    k_aug = AE(nc.sbuf_tensor("k_aug", [128, S], BF16))
    q_aug = AE(nc.sbuf_tensor("q_aug", [128, S], BF16))
    Pt = AE(nc.sbuf_tensor("Pt", [128, S], BF16))
    negmax = AE(nc.sbuf_tensor("negmax", [128, 16], F32))
    flat_src = AE(nc.sbuf_tensor("flat_src", [16, 128], F32))
    invZ_bf = AE(nc.sbuf_tensor("invZ_bf", [1, S], BF16))
    bz_sb = Pt[0:64, :]
    pvctx = Pt[64:128, :]
    ctx2u = AE(nc.sbuf_tensor("ctx2u", [128, 1024], BF16))
    ctxsq = xsq

    if stop_after != "mod":
        with nc.Block("battinit") as block:
            @block.gpsimd
            def _(g):
                for w_sb, w_dd in ((wq_sb, wq_d), (wk_sb, wk_d), (wv_sb, wv_d), (wf_sb, wf_d)):
                    s_ld.inc(g.dma_start(w_sb[:], w_dd), 16)
                for dst, src in ((bq_row, bq_d), (bk_row, bk_d), (bv_row, bv_d), (bf_row, bf_d)):
                    s_ld.inc(g.dma_start(dst[:], src), 16)
            @block.vector
            def _(v):
                if stop_after != "battnomemset":
                    v.memset(vaug[:], 0.0)
                    va = vaug[:, 0:UPG * 16 * 65].rearrange(
                        "p (u f j) -> p u f j", f=16, j=65)
                    v.memset(va[:, :, :, 64:65], 1.0)
                    v.memset(k_aug[64:128, :], 0.0)
                    v.memset(k_aug[64:65, :], 1.0)
                    v.memset(q_aug[64:128, :], 0.0)
                    v.memset(kdA[64:128, :], 0.0)
                    v.memset(kdB[0:64, :], 0.0)
    LD_W = s_ld.n

    def qkv_group(g):
        t0 = g * TG
        with nc.Block(f"qkv_{g}") as block:
            ln_group(f"ln1_{g}", g, x_sb, h1g, True, ca1_t, sa_t, xsq, block=block)
            M[("h1done", g)] = s_vec.n
            with (nc.psum_tensor(f"ps_q0_{g}", [128, 512], F32) as ps_q0,
                  nc.psum_tensor(f"ps_q1_{g}", [128, 512], F32) as ps_q1,
                  nc.psum_tensor(f"ps_v0_{g}", [128, 1024], F32) as ps_v0,
                  nc.psum_tensor(f"ps_v1_{g}", [128, 1024], F32) as ps_v1):
                ps_q = [ps_q0, ps_q1]
                ps_v = [ps_v0, ps_v1]
                mmb, actb, vecb = s_mm.n, s_act.n, s_vec.n
                @block.tensor
                def _(t):
                    t.wait_ge(s_ld.h, LD_W)
                    t.wait_ge(s_vec.h, M[("h1done", g)])
                    for wi, (w_sb, b_row) in enumerate(((wq_sb, bq_row), (wk_sb, bk_row))):
                        for oc in range(NCH):
                            idx = wi * 8 + oc
                            if idx >= 2:
                                t.wait_ge(s_act.h, actb + idx - 1)
                            p = ps_q[idx % 2]
                            for dc in range(NCH):
                                t.matmul(p[:], w_sb[:, dc * D + oc * 128: dc * D + (oc + 1) * 128],
                                         h1g[:, dc * TG: (dc + 1) * TG],
                                         start=(dc == 0), stop=False)
                            s_mm.inc(t.matmul(p[:], b_row[0:1, oc * 128:(oc + 1) * 128],
                                     ones_row[0:1, :], start=False, stop=True))
                    for tc in range(4):
                        if tc >= 2:
                            t.wait_ge(s_vec.h, vecb + tc - 1)
                        p = ps_v[tc % 2]
                        for dc in range(NCH):
                            lhs = h1g[:, dc * TG + tc * 128: dc * TG + (tc + 1) * 128]
                            for h in range(2):
                                t.matmul(p[:, h * 512:(h + 1) * 512], lhs,
                                         wv_sb[:, dc * D + h * 512: dc * D + (h + 1) * 512],
                                         start=(dc == 0), stop=False)
                        last = None
                        for h in range(2):
                            last = t.matmul(p[:, h * 512:(h + 1) * 512], ones_row[0:1, 0:128],
                                     bv_row[0:1, h * 512:(h + 1) * 512],
                                     start=False, stop=True)
                        s_mm.inc(last)
                @block.scalar
                def _(sc):
                    for wi, dst in enumerate((qg, kg)):
                        for oc in range(NCH):
                            idx = wi * 8 + oc
                            sc.wait_ge(s_mm.h, mmb + idx + 1)
                            s_act.inc(sc.activation(dst[:, oc * TG:(oc + 1) * TG],
                                                    ps_q[idx % 2][:], AF.Copy,
                                                    bias=0.0,
                                                    scale=(8.0 if wi == 0 else 1.0)))
                @block.vector
                def _(v):
                    va = vaug[:, 0:UPG * 16 * 65].rearrange(
                        "p (u f j) -> p u f j", f=16, j=65)
                    for tc in range(4):
                        v.wait_ge(s_mm.h, mmb + 16 + tc + 1)
                        src = ps_v[tc % 2][:].rearrange("p (f j) -> p f j", j=64)
                        s_vec.inc(v.tensor_copy(va[:, tc, :, 0:64], src))
                if g == 0 and any(nm in taps for nm in ("h1", "qg", "kg", "vaug")):
                    @block.sync
                    def _(sy):
                        sy.wait_ge(s_act.h, s_act.n)
                        sy.wait_ge(s_vec.h, s_vec.n)
                        for nm, bufap in (("h1", h1g[:]), ("qg", qg[:]), ("kg", kg[:]),
                                          ("vaug", vaug[:, 0:UPG * 16 * 65])):
                            if nm in taps:
                                s_out.inc(sy.dma_start(tap_d[nm], bufap), 16)

            u = g * UPG
            go = 0
            mm_q = s_mm.n
            act_q = s_act.n
            vec_q = s_vec.n
            with (nc.psum_tensor(f"psA_{u}", [128, S], F32) as psA,
                  nc.psum_tensor(f"psB_{u}", [128, S], F32) as psB):
                @block.gpsimd
                def _(g_):
                    g_.wait_ge(s_act.h, act_q)
                    emit_repacks(g_, u)
                    g_.wait_ge(s_act.h, act_q + 1)
                    s_rep.inc(g_.dma_start(q_aug[64:65, :], flat_src[:]), 16)
                    M[("flat", u)] = s_rep.n
                @block.tensor
                def _(t):
                    t.wait_ge(s_rep.h, M[("kdup", u)])
                    t.wait_ge(s_vec.h, vec_q)
                    for j in range(NCH):
                        if j >= 1:
                            t.wait_ge(s_vec.h, vec_q + 2 * j)
                        lqj = qg[0:128, j * TG + go: j * TG + go + 128]
                        for n in range(4):
                            t.matmul(psA[:, n * 512:(n + 1) * 512], lqj,
                                     kdA[0:128, n * 512:(n + 1) * 512],
                                     start=True, stop=True)
                        last = None
                        for n in range(4):
                            last = t.matmul(psB[:, n * 512:(n + 1) * 512], lqj,
                                     kdB[0:128, n * 512:(n + 1) * 512],
                                     start=True, stop=True)
                        s_mm.inc(last)
                @block.vector
                def _(v):
                    for j in range(NCH):
                        v.wait_ge(s_mm.h, mm_q + j + 1)
                        s_vec.inc(v.reduce_max(negmax[:, 2 * j:2 * j + 1], psA[:],
                                               axis=AX.X, negate=True))
                        s_vec.inc(v.reduce_max(negmax[:, 2 * j + 1:2 * j + 2], psB[:],
                                               axis=AX.X, negate=True))
                    M[("reds", u)] = s_vec.n
            with nc.psum_tensor(f"ps_nm_{u}", [16, 128], F32) as ps_nm:
                mmb2 = s_mm.n
                @block.tensor
                def _(t):
                    t.wait_ge(s_vec.h, M[("reds", u)])
                    s_mm.inc(t.transpose(ps_nm[:], negmax[:], ident[:]))
                @block.scalar
                def _(sc):
                    sc.wait_ge(s_mm.h, mmb2 + 1)
                    s_act.inc(sc.copy(flat_src[:], ps_nm[:]))
                    M[("flatevac", u)] = s_act.n

    zt = AE(nc.sbuf_tensor("zt", [128, 16], F32))
    zt2 = AE(nc.sbuf_tensor("zt2", [128, 16], F32))
    negmaxA = AE(nc.sbuf_tensor("negmaxA", [128, 16], F32))
    negmaxB = AE(nc.sbuf_tensor("negmaxB", [128, 16], F32))
    M = {}

    def emit_repacks(g_, u):
        g = u // UPG
        go = (u % UPG) * 128
        kdAv = kdA[:].rearrange("p (c g2 t) -> p c g2 t", g2=2, t=128)
        kdBv = kdB[:].rearrange("p (c g2 t) -> p c g2 t", g2=2, t=128)
        kav = k_aug[:].rearrange("p (c g2 t) -> p c g2 t", g2=2, t=128)
        qav = q_aug[:].rearrange("p (c g2 t) -> p c g2 t", g2=2, t=128)
        kgv_t = kg[0:64, :].rearrange("p (c t) -> p c t", t=TG)[:, :, go:go + 128]
        kgv_b = kg[64:128, :].rearrange("p (c t) -> p c t", t=TG)[:, :, go:go + 128]
        qgv_t = qg[0:64, :].rearrange("p (c t) -> p c t", t=TG)[:, :, go:go + 128]
        qgv_b = qg[64:128, :].rearrange("p (c t) -> p c t", t=TG)[:, :, go:go + 128]
        for dst, srcv in ((kdAv[0:64, :, 0, :], kgv_t), (kdAv[0:64, :, 1, :], kgv_b),
                          (kdBv[64:128, :, 0, :], kgv_t), (kdBv[64:128, :, 1, :], kgv_b)):
            s_rep.inc(g_.dma_start(dst, srcv), 16)
        M[("kdup", u)] = s_rep.n
        for dst, srcv in ((kav[0:64, :, 0, :], kgv_t), (kav[0:64, :, 1, :], kgv_b),
                          (qav[0:64, :, 0, :], qgv_t), (qav[0:64, :, 1, :], qgv_b)):
            s_rep.inc(g_.dma_start(dst, srcv), 16)


    pv_all = AE(nc.sbuf_tensor("pv_all", [65, UPG * S], BF16))
    ctxn_g = AE(nc.sbuf_tensor("ctxn_g", [128, NCH * TG], BF16))

    def attz_block(block, g, uo):
        u = g * UPG + uo
        nxt = u + 1 if uo < UPG - 1 else None
        ngo = (uo + 1) * 128
        if True:
            with (nc.psum_tensor(f"zpv_{u}", [128, S], F32) as ps_PV,
                  nc.psum_tensor(f"zst_{u}", [128, 1024], F32) as ps_St,
                  nc.psum_tensor(f"zs_{u}", [128, 1024], F32) as ps_Sz):
                mm0, act0, vec0 = s_mm.n, s_act.n, s_vec.n
                per_f = 6 if nxt is not None else 4
                def mm_st(f, h):   return mm0 + per_f * f + (1 if h == 0 else (3 if nxt is not None else 2))
                def mm_sz(f, h):   return mm0 + per_f * f + (2 if h == 0 else 4)
                def mm_pv(f, h):   return mm0 + per_f * f + (per_f - 1 if h == 0 else per_f)
                def act_exp(f, h): return act0 + 2 * f + h + 1
                def vec_red(f, h): return vec0 + 2 * f + h + 1
                tE = mm0 + 16 * per_f
                if nxt is not None:
                    M[("nmtr", nxt)] = tE + 1
                    M[("reds", nxt)] = vec0 + 33
                M[("st15", u)] = mm_st(15, 1)
                if nxt is not None:
                    M[("szlast", u)] = mm_sz(15, 1)
                M[("pvdone", u)] = mm_pv(15, 1)
                M[("expdone", u)] = act0 + 32
                M[("pvevac", u)] = act0 + 33
                if nxt is not None:
                    M[("flatevac", nxt)] = act0 + 34

                @block.gpsimd
                def _(g_):
                    if nxt is not None:
                        if uo >= 1:
                            g_.wait_ge(s_mm.h, M[("szlast", u - 1)])
                        kdAv = kdA[:].rearrange("p (c g2 t) -> p c g2 t", g2=2, t=128)
                        kdBv = kdB[:].rearrange("p (c g2 t) -> p c g2 t", g2=2, t=128)
                        kgv_t = kg[0:64, :].rearrange("p (c t) -> p c t", t=TG)[:, :, ngo:ngo + 128]
                        kgv_b = kg[64:128, :].rearrange("p (c t) -> p c t", t=TG)[:, :, ngo:ngo + 128]
                        for dst, srcv in ((kdAv[0:64, :, 0, :], kgv_t), (kdAv[0:64, :, 1, :], kgv_b),
                                          (kdBv[64:128, :, 0, :], kgv_t), (kdBv[64:128, :, 1, :], kgv_b)):
                            s_rep.inc(g_.dma_start(dst, srcv), 16)
                        M[("kdup", nxt)] = s_rep.n
                        g_.wait_ge(s_mm.h, M[("st15", u)])
                        kav = k_aug[:].rearrange("p (c g2 t) -> p c g2 t", g2=2, t=128)
                        qav = q_aug[:].rearrange("p (c g2 t) -> p c g2 t", g2=2, t=128)
                        qgv_t = qg[0:64, :].rearrange("p (c t) -> p c t", t=TG)[:, :, ngo:ngo + 128]
                        qgv_b = qg[64:128, :].rearrange("p (c t) -> p c t", t=TG)[:, :, ngo:ngo + 128]
                        for dst, srcv in ((kav[0:64, :, 0, :], kgv_t), (kav[0:64, :, 1, :], kgv_b),
                                          (qav[0:64, :, 0, :], qgv_t), (qav[0:64, :, 1, :], qgv_b)):
                            s_rep.inc(g_.dma_start(dst, srcv), 16)
                        g_.wait_ge(s_act.h, M[("flatevac", nxt)])
                        s_rep.inc(g_.dma_start(q_aug[64:65, :], flat_src[:]), 16)
                        M[("flat", nxt)] = s_rep.n

                @block.tensor
                def _(t):
                    t.wait_ge(s_rep.h, M[("flat", u)])
                    t.wait_ge(s_act.h, M.get(("expdone", u - 1), 0))
                    for f in range(16):
                        for h in range(2):
                            if f >= 1 or h == 1:
                                t.wait_ge(s_act.h, act_exp(f - 1 + h, 1 - h))
                            last = None
                            for n in range(2):
                                rq = h * 1024 + n * 512
                                last = t.matmul(ps_St[:, n * 512:(n + 1) * 512],
                                         k_aug[0:128, f * 128:(f + 1) * 128],
                                         q_aug[0:128, rq:rq + 512],
                                         start=True, stop=True)
                            s_mm.inc(last)
                            if nxt is not None:
                                if f >= 1 or h == 1:
                                    t.wait_ge(s_vec.h, vec_red(f - 1 + h, 1 - h))
                                if f == 0 and h == 0:
                                    t.wait_ge(s_rep.h, M[("kdup", nxt)])
                                    t.wait_ge(s_act.h, M[("flatevac", u)])
                                kd = kdA if f % 2 == 0 else kdB
                                lq = qg[0:128, (f // 2) * TG + ngo: (f // 2) * TG + ngo + 128]
                                last = None
                                for n in range(2):
                                    rp = h * 1024 + n * 512
                                    last = t.matmul(ps_Sz[:, n * 512:(n + 1) * 512], lq,
                                             kd[0:128, rp:rp + 512],
                                             start=True, stop=True)
                                s_mm.inc(last)
                        for h in range(2):
                            t.wait_ge(s_act.h, act_exp(f, h))
                            if f == 0 and h == 0:
                                t.wait_ge(s_act.h, M.get(("pvevac", u - 1), 0))
                            last = None
                            for n in range(2):
                                rq = h * 1024 + n * 512
                                last = t.matmul(ps_PV[:, rq:rq + 512],
                                         vaug[:, uo * 1040 + f * 65: uo * 1040 + f * 65 + 128],
                                         Pt[:, rq:rq + 512],
                                         start=(f == 0), stop=(f == 15))
                            s_mm.inc(last)
                    if nxt is not None:
                        t.wait_ge(s_vec.h, M[("reds", nxt)])
                        s_mm.inc(t.transpose(ps_Sz[0:16, 512:640], negmax[:], ident[:]))

                @block.scalar
                def _(sc):
                    for f in range(16):
                        for h in range(2):
                            sc.wait_ge(s_mm.h, mm_st(f, h))
                            if f == 0 and h == 0:
                                sc.wait_ge(s_mm.h, M.get(("pvdone", u - 1), 0))
                            if f >= 1:
                                sc.wait_ge(s_mm.h, mm_pv(f - 1, h))
                            s_act.inc(sc.activation(Pt[:, h * 1024:(h + 1) * 1024],
                                                    ps_St[:], AF.Exp))
                    sc.wait_ge(s_mm.h, M[("pvdone", u)])
                    s_act.inc(sc.activation(pv_all[:, uo * S:(uo + 1) * S],
                                            ps_PV[0:65, :], AF.Copy))
                    if nxt is not None:
                        sc.wait_ge(s_mm.h, M[("nmtr", nxt)])
                        s_act.inc(sc.copy(flat_src[:], ps_Sz[0:16, 512:640]))

                @block.vector
                def _(v):
                    if nxt is not None:
                        for f in range(16):
                            for h in range(2):
                                v.wait_ge(s_mm.h, mm_sz(f, h))
                                tgt = negmaxA if h == 0 else negmaxB
                                s_vec.inc(v.reduce_max(tgt[:, f:f + 1], ps_Sz[:],
                                                       axis=AX.X, negate=True))
                        v.tensor_copy(scr8[:], negmaxA[:, 0:8])
                        s_vec.inc(v.tensor_tensor(negmax[:], negmaxA[:], negmaxB[:],
                                                  ALU.min))
                        assert s_vec.n == M[("reds", nxt)]

    def attepi_block(block, g):
        if True:
            with (nc.psum_tensor(f"ebz_{g}", [64, S], F32) as ps_bz,
                  nc.psum_tensor(f"eln_{g}", [128, 512], F32) as ps_ln,
                  nc.psum_tensor(f"ef0_{g}", [128, 512], F32) as ps_f0,
                  nc.psum_tensor(f"ef1_{g}", [128, 512], F32) as ps_f1):
                ps_F = [ps_f0, ps_f1]
                mm0, act0, vec0, rep0 = s_mm.n, s_act.n, s_vec.n, s_rep.n
                A_ctxsq = [act0 + 2 * i + 1 for i in range(UPG)]
                A_rstd = [act0 + 2 * i + 2 for i in range(UPG)]
                T_bz = [mm0 + 3 * i + 1 for i in range(UPG)]
                T_ln = [mm0 + 3 * i + 2 for i in range(UPG)]
                T_b2 = [mm0 + 3 * i + 3 for i in range(UPG)]
                T_f = [mm0 + 3 * UPG + oc + 1 for oc in range(NCH)]
                V_rec = [vec0 + 6 * i + 1 for i in range(UPG)]
                V_pvm = [vec0 + 6 * i + 2 for i in range(UPG)]
                V_ctx2 = [vec0 + 6 * i + 3 for i in range(UPG)]
                V_var = [vec0 + 6 * i + 4 for i in range(UPG)]
                V_nmr = [vec0 + 6 * i + 5 for i in range(UPG)]
                V_ctxn = [vec0 + 6 * i + 6 for i in range(UPG)]
                V_fev = [vec0 + 6 * UPG + oc + 1 for oc in range(NCH)]
                R_z1 = [rep0 + 64 * i + 16 for i in range(UPG)]
                R_invz = [rep0 + 64 * i + 32 for i in range(UPG)]
                R_remap = [rep0 + 64 * i + 64 for i in range(UPG)]

                @block.scalar
                def _(sc):
                    for i in range(UPG):
                        sc.wait_ge(s_vec.h, V_ctx2[i])
                        s_act.inc(sc.activation(ctxsq[:], ctx2u[:], AF.Square))
                        sc.wait_ge(s_vec.h, V_var[i])
                        sc.activation(lnv_r[0:1, 0:128], var_r[0:1, 0:128], AF.Ln)
                        s_act.inc(sc.activation(rstd_bf[0:1, 0:128], lnv_r[0:1, 0:128],
                                                AF.Exp, bias=0.0, scale=-0.5))

                @block.tensor
                def _(t):
                    t.wait_ge(s_act.h, M[("expdone", g * UPG + UPG - 1)])
                    t.wait_ge(s_act.h, M[("pvevac", g * UPG + UPG - 1)])
                    for i in range(UPG):
                        t.wait_ge(s_rep.h, R_invz[i])
                        if i >= 1:
                            t.wait_ge(s_vec.h, V_pvm[i - 1])
                        last = None
                        for n in range(4):
                            last = t.matmul(ps_bz[:, n * 512:(n + 1) * 512],
                                     ones_row[0:1, 0:64],
                                     invZ_bf[0:1, n * 512:(n + 1) * 512],
                                     start=True, stop=True)
                        s_mm.inc(last)
                        t.wait_ge(s_vec.h, V_ctx2[i])
                        if i >= 1:
                            t.wait_ge(s_vec.h, V_ctxn[i - 1])
                        for c in range(NCH):
                            t.matmul(ps_ln[0:1, 0:128], ones_col[:],
                                     ctx2u[:, c * 128:(c + 1) * 128],
                                     start=(c == 0), stop=(c == NCH - 1))
                        t.wait_ge(s_act.h, A_ctxsq[i])
                        last = None
                        for c in range(NCH):
                            last = t.matmul(ps_ln[0:1, 128:256], ones_col[:],
                                     ctxsq[:, c * 128:(c + 1) * 128],
                                     start=(c == 0), stop=(c == NCH - 1))
                        s_mm.inc(last)
                        t.wait_ge(s_vec.h, V_nmr[i])
                        t.matmul(ps_ln[:, 256:384], ones_row[0:1, 0:128],
                                 rstd_bf[0:1, 0:128], start=True, stop=True)
                        s_mm.inc(t.matmul(ps_ln[:, 384:512], ones_row[0:1, 0:128],
                                 nmr_bf[0:1, 0:128], start=True, stop=True))
                    t.wait_ge(s_ld.h, LD_W)
                    for oc in range(NCH):
                        p = ps_F[oc % 2]
                        t.wait_ge(s_vec.h, V_ctxn[UPG - 1])
                        if oc >= 2:
                            t.wait_ge(s_vec.h, V_fev[oc - 2])
                        for dc in range(NCH):
                            t.matmul(p[:], wf_sb[:, dc * D + oc * 128: dc * D + (oc + 1) * 128],
                                     ctxn_g[:, dc * TG:(dc + 1) * TG],
                                     start=(dc == 0), stop=False)
                        s_mm.inc(t.matmul(p[:], bf_row[0:1, oc * 128:(oc + 1) * 128],
                                 ones_row[0:1, :], start=False, stop=True))

                @block.vector
                def _(v):
                    for i in range(UPG):
                        u = g * UPG + i
                        go_i = i * 128
                        v.wait_ge(s_rep.h, R_z1[i])
                        with nc.allow_low_precision(reason="1/Z softmax scale via bf16 round-trip, within tolerance"):
                            s_vec.inc(v.reciprocal(zt2[:], zt[:]))
                        v.wait_ge(s_mm.h, T_bz[i])
                        if i >= 1:
                            v.wait_ge(s_rep.h, R_remap[i - 1])
                        s_vec.inc(v.tensor_mul(pvctx[:], pv_all[0:64, i * S:(i + 1) * S],
                                               ps_bz[:]))
                        v.wait_ge(s_rep.h, R_remap[i])
                        h1v = h1g[:].rearrange("p (c t) -> p c t", t=TG)[:, :, go_i:go_i + 128]
                        c2v = ctx2u[:].rearrange("p (c t) -> p c t", t=128)
                        s_vec.inc(v.tensor_tensor(c2v, c2v, h1v, ALU.add))
                        v.wait_ge(s_mm.h, T_ln[i])
                        v.tensor_scalar(mean_r[0:1, 0:128], ps_ln[0:1, 0:128],
                                        INV_D, None, ALU.mult)
                        v.tensor_mul(msq_r[0:1, 0:128], mean_r[0:1, 0:128],
                                     mean_r[0:1, 0:128])
                        v.tensor_scalar(msq_r[0:1, 0:128], msq_r[0:1, 0:128],
                                        1.0, -EPS, ALU.mult, ALU.add)
                        s_vec.inc(v.scalar_tensor_tensor(var_r[0:1, 0:128],
                                                         ps_ln[0:1, 128:256], INV_D,
                                                         msq_r[0:1, 0:128],
                                                         ALU.mult, ALU.subtract))
                        v.wait_ge(s_act.h, A_rstd[i])
                        s_vec.inc(v.scalar_tensor_tensor(nmr_bf[0:1, 0:128],
                                                         mean_r[0:1, 0:128], -1.0,
                                                         rstd_bf[0:1, 0:128],
                                                         ALU.mult, ALU.mult))
                        v.wait_ge(s_mm.h, T_b2[i])
                        for c in range(NCH):
                            v.tensor_tensor(ctxn_g[:, c * TG + go_i: c * TG + go_i + 128],
                                            ctx2u[:, c * 128:(c + 1) * 128],
                                            ps_ln[:, 256:384], ALU.mult)
                            ins = v.tensor_tensor(ctxn_g[:, c * TG + go_i: c * TG + go_i + 128],
                                                  ctxn_g[:, c * TG + go_i: c * TG + go_i + 128],
                                                  ps_ln[:, 384:512], ALU.add)
                            if c == NCH - 1:
                                s_vec.inc(ins)
                    for oc in range(NCH):
                        v.wait_ge(s_mm.h, T_f[oc])
                        xs = x_sb[:, oc * S + g * TG: oc * S + (g + 1) * TG]
                        s_vec.inc(v.scalar_tensor_tensor(
                            xs, ps_F[oc % 2][:], ga_t[:, oc:oc + 1], xs,
                            ALU.mult, ALU.add))

                @block.gpsimd
                def _(g_):
                    for i in range(UPG):
                        u = g * UPG + i
                        if i >= 1:
                            g_.wait_ge(s_vec.h, V_rec[i - 1])
                        g_.wait_ge(s_act.h, M[("pvevac", u)])
                        s_rep.inc(g_.dma_start(zt[:], pv_all[64:65, i * S:(i + 1) * S]), 16)
                        g_.wait_ge(s_vec.h, V_rec[i])
                        if i >= 1:
                            g_.wait_ge(s_mm.h, T_bz[i - 1])
                        s_rep.inc(g_.dma_start(invZ_bf[:], zt2[:]), 16)
                        g_.wait_ge(s_vec.h, V_pvm[i])
                        if i >= 1:
                            g_.wait_ge(s_vec.h, V_ctxn[i - 1])
                        pv_e = pvctx[:].rearrange("p (c f2 t) -> p c f2 t", f2=2, t=128)
                        c2 = ctx2u[:].rearrange("p (c t) -> p c t", t=128)
                        s_rep.inc(g_.dma_start(c2[0:64, :, :], pv_e[:, :, 0, :]), 16)
                        s_rep.inc(g_.dma_start(c2[64:128, :, :], pv_e[:, :, 1, :]), 16)

                if g == 0 and any(nm in taps for nm in ("ctx2", "ctxn")):
                    @block.sync
                    def _(sy):
                        if "ctx2" in taps:
                            sy.wait_ge(s_vec.h, V_ctx2[0])
                            s_out.inc(sy.dma_start(tap_d["ctx2"], ctx2u[:]), 16)
                        if "ctxn" in taps:
                            sy.wait_ge(s_vec.h, V_ctxn[0])
                            s_out.inc(sy.dma_start(tap_d["ctxn"], ctxn_g[:, 0:1024]), 16)


    if stop_after not in ("mod", "battinit", "battnomemset"):
        for g in range(NG):
            if stop_after == "ln1":
                break
            qkv_group(g)
            if stop_after == "qkv":
                break
            with nc.Block(f"attg_{g}") as ab:
                for uo in range(UPG):
                    attz_block(ab, g, uo)
                attepi_block(ab, g)
            if stop_after in ("unit0", "alpha0"):
                break

    att_scope.close()
    _skip_moe = stop_after in ("mod", "battinit", "battnomemset", "ln1", "qkv", "unit0", "alpha0", "att")

    if "x2" in taps:
        with nc.Block("tapx2") as block:
            @block.sync
            def _(sy):
                s_out.inc(sy.dma_start(tap_d["x2"], x_sb[:]), 16)

    moe_pre = contextlib.ExitStack()
    MP = moe_pre.enter_context
    h2 = E(nc.sbuf_tensor("h2", [128, NCH * S], BF16))
    coefT_bf = E(nc.sbuf_tensor("coefT_bf", [NE, S], BF16))
    sel_sb = E(nc.sbuf_tensor("sel_sb", [NE, NE * 128], BF16))
    xsq2 = MP(nc.sbuf_tensor("xsq2", [128, 1024], BF16))
    logits_s = MP(nc.sbuf_tensor("logits_s", [NE, S], F32))
    logits_tm = MP(nc.sbuf_tensor("logits_tm", [128, 128], F32))
    coef_tm = MP(nc.sbuf_tensor("coef_tm", [128, 128], F32))
    nm1 = MP(nc.sbuf_tensor("nm1", [128, 1], F32))
    z1 = MP(nc.sbuf_tensor("z1", [128, 1], F32))
    w1 = MP(nc.sbuf_tensor("w1", [128, 1], F32))
    scr_e = MP(nc.sbuf_tensor("scr_e", [128, 8], F32))
    scr_ge = MP(nc.sbuf_tensor("scr_ge", [128, 8], F32))
    scr_cu = MP(nc.sbuf_tensor("scr_cu", [128, 8], F32))
    scr_eq = MP(nc.sbuf_tensor("scr_eq", [128, 8], F32))
    scr_m = MP(nc.sbuf_tensor("scr_m", [128, 8], F32))

    if not _skip_moe:
     for g in range(NG):
        ln_group(f"ln3_{g}", g, x_sb, h2, False, cb1_t, sb_t, xsq2)

    if "h2t" in taps:
        with nc.Block("taph2") as block:
            @block.sync
            def _(sy):
                s_out.inc(sy.dma_start(tap_d["h2t"], h2[:]), 16)

    if not _skip_moe:
     with nc.Block("router") as block:
        with (nc.psum_tensor("ps_lg", [NE, S], F32) as ps_lg,
              nc.psum_tensor("ps_ltm", [128, 128], F32) as ps_ltm):
            mmb, actb = s_mm.n, s_act.n
            @block.tensor
            def _(t):
                t.wait_ge(s_ld.h, LD_SMALL)
                for n in range(4):
                    for dc in range(NCH):
                        t.matmul(ps_lg[:, n * 512:(n + 1) * 512],
                                 rwT_sb[:, dc * NE:(dc + 1) * NE],
                                 h2[:, dc * S + n * 512: dc * S + (n + 1) * 512],
                                 start=(dc == 0), stop=False)
                    last = t.matmul(ps_lg[:, n * 512:(n + 1) * 512], rb_row[:],
                             ones_row[0:1, :], start=False, stop=True)
                s_mm.inc(last)
                t.wait_ge(s_act.h, actb + 1)
                for tch in range(16):
                    last = t.transpose(ps_ltm[:, tch * 8:(tch + 1) * 8],
                                logits_s[0:NE, tch * 128:(tch + 1) * 128],
                                ident[0:NE, 0:NE])
                s_mm.inc(last)
            @block.scalar
            def _(sc):
                sc.wait_ge(s_mm.h, mmb + 1)
                s_act.inc(sc.activation(logits_s[:], ps_lg[:], AF.Copy))
                sc.wait_ge(s_mm.h, mmb + 2)
                s_act.inc(sc.copy(logits_tm[:], ps_ltm[:]))
            if "ltm" in taps:
                @block.sync
                def _(sy):
                    sy.wait_ge(s_act.h, s_act.n)
                    s_out.inc(sy.dma_start(tap_d["ltm"], logits_tm[:]), 16)

    if not _skip_moe:
     with nc.Block("coef") as block:
        vecb, actb = s_vec.n, s_act.n
        @block.vector
        def _(v):
            for tch in range(16):
                sl = logits_tm[:, tch * 8:(tch + 1) * 8]
                if tch >= 1:
                    v.wait_ge(s_act.h, actb + tch)
                s_vec.inc(v.reduce_max(nm1[:], sl, axis=AX.X, negate=True))
                v.wait_ge(s_act.h, actb + tch + 1)
                s_vec.inc(v.reciprocal(w1[:], z1[:]))
                v.tensor_scalar(scr_ge[:], sl, nm1[:], 0.0, ALU.add, ALU.is_ge)
                v.tensor_tensor_scan(scr_cu[:], scr_ge[:], scr_ge[:], 0.0,
                                     ALU.add, ALU.bypass)
                v.tensor_scalar(scr_eq[:], scr_cu[:], 1.0, None, ALU.is_equal)
                v.tensor_mul(scr_m[:], scr_ge[:], scr_eq[:])
                s_vec.inc(v.tensor_scalar(coef_tm[:, tch * 8:(tch + 1) * 8],
                                          scr_m[:], w1[:], None, ALU.mult))
        @block.scalar
        def _(sc):
            for tch in range(16):
                sc.wait_ge(s_vec.h, vecb + 3 * tch + 1)
                if tch >= 1:
                    sc.wait_ge(s_vec.h, vecb + 3 * (tch - 1) + 2)
                s_act.inc(sc.activation(scr_e[:], logits_tm[:, tch * 8:(tch + 1) * 8],
                                        AF.Exp, bias=nm1[:], scale=1.0,
                                        accum_out=z1[:]))
        if "coef" in taps:
            @block.sync
            def _(sy):
                sy.wait_ge(s_vec.h, s_vec.n)
                s_out.inc(sy.dma_start(tap_d["coef"], coef_tm[:]), 16)

    if not _skip_moe:
     with nc.Block("coefT") as block:
        with nc.psum_tensor("ps_cT", [NE, S], F32) as ps_cT:
            mmb = s_mm.n
            @block.tensor
            def _(t):
                last = None
                for tch in range(16):
                    last = t.transpose(ps_cT[0:NE, tch * 128:(tch + 1) * 128],
                                coef_tm[:, tch * 8:(tch + 1) * 8], ident[:])
                s_mm.inc(last)
            @block.scalar
            def _(sc):
                sc.wait_ge(s_mm.h, mmb + 1)
                s_act.inc(sc.activation(coefT_bf[:], ps_cT[:], AF.Copy))
        @block.sync
        def _(sy):
            s_ld.inc(sy.dma_start(sel_sb[:], sel_d.ap()), 16)
    moe_pre.close()

    moe_scope = contextlib.ExitStack()
    ME = moe_scope.enter_context
    Be_tc = ME(nc.sbuf_tensor("Be_tc", [128, NE * 512], BF16))
    ew_sb = [ME(nc.sbuf_tensor(f"ew_sb{i}", [128, NCH * D], BF16)) for i in range(2)]
    h2s = [ME(nc.sbuf_tensor(f"h2s{i}", [128, NCH * 512], BF16)) for i in range(2)]
    out_stage = ME(nc.sbuf_tensor("out_stage", [128, 4096], F32))

    if not _skip_moe:
     with nc.Block("experts") as block:
        with nc.psum_tensor("ps_moe", [128, 4096], F32) as ps_moe:
            mmb, vecb, actb, ewb, outb = s_mm.n, s_vec.n, s_act.n, s_ew.n, s_out.n
            @block.gpsimd
            def _(g):
                for tc in range(4):
                    for e in range(NE):
                        i = tc * NE + e
                        if i >= 2:
                            i2 = i - 2
                            g.wait_ge(s_mm.h, mmb + (i2 // NE) * 17 + 8 + (i2 % NE) + 1)
                        s_ew.inc(g.dma_start(ew_sb[i % 2][:], ew_d[e]), 16)
            @block.sync
            def _(sy):
                for tc in range(4):
                    for oc in range(NCH):
                        sy.wait_ge(s_vec.h, vecb + tc * 24 + 16 + oc + 1)
                        s_out.inc(sy.dma_start(
                            out_d[oc * 128:(oc + 1) * 128, tc * 512:(tc + 1) * 512],
                            out_stage[:, oc * 512:(oc + 1) * 512]), 16)
            @block.tensor
            def _(t):
                t.wait_ge(s_ld.h, s_ld.n)
                for tc in range(4):
                    for e in range(NE):
                        slot = (e % 2) * 512
                        if e >= 2:
                            t.wait_ge(s_vec.h, vecb + tc * 24 + e - 1)
                        if tc >= 1 and e < 2:
                            t.wait_ge(s_vec.h, vecb + (tc - 1) * 24 + 24)
                        s_mm.inc(t.matmul(ps_moe[:, slot:slot + 512],
                                 sel_sb[0:NE, e * 128:(e + 1) * 128],
                                 coefT_bf[0:NE, tc * 512:(tc + 1) * 512],
                                 start=True, stop=True))
                    for e in range(NE):
                        t.wait_ge(s_ew.h, ewb + (tc * NE + e + 1) * 16)
                        t.wait_ge(s_vec.h, vecb + tc * 24 + 8 + e + 1)
                        buf = ew_sb[(tc * NE + e) % 2]
                        hbuf = h2s[e % 2]
                        last = None
                        for oc in range(NCH):
                            for dc in range(NCH):
                                last = t.matmul(ps_moe[:, oc * 512:(oc + 1) * 512],
                                         buf[:, dc * D + oc * 128: dc * D + (oc + 1) * 128],
                                         hbuf[:, dc * 512:(dc + 1) * 512],
                                         start=(e == 0 and dc == 0), stop=False)
                        s_mm.inc(last)
                    last = None
                    for oc in range(NCH):
                        last = t.matmul(ps_moe[:, oc * 512:(oc + 1) * 512],
                                 eb_sb[0:NE, oc * 128:(oc + 1) * 128],
                                 coefT_bf[0:NE, tc * 512:(tc + 1) * 512],
                                 start=False, stop=True)
                    s_mm.inc(last)
            @block.vector
            def _(v):
                for tc in range(4):
                    for e in range(NE):
                        slot = (e % 2) * 512
                        v.wait_ge(s_mm.h, mmb + tc * 17 + e + 1)
                        s_vec.inc(v.tensor_copy(Be_tc[:, e * 512:(e + 1) * 512],
                                                ps_moe[:, slot:slot + 512]))
                    for e in range(NE):
                        hbuf = h2s[e % 2]
                        if e >= 2:
                            v.wait_ge(s_mm.h, mmb + tc * 17 + 8 + e - 1)
                        for dc in range(NCH):
                            ins = v.tensor_tensor(
                                hbuf[:, dc * 512:(dc + 1) * 512],
                                h2[:, dc * S + tc * 512: dc * S + (tc + 1) * 512],
                                Be_tc[:, e * 512:(e + 1) * 512], ALU.mult)
                            if dc == NCH - 1:
                                s_vec.inc(ins)
                    v.wait_ge(s_mm.h, mmb + tc * 17 + 17)
                    if tc >= 1:
                        v.wait_ge(s_out.h, outb + tc * NCH * 16)
                    for oc in range(NCH):
                        xs = x_sb[:, oc * S + tc * 512: oc * S + (tc + 1) * 512]
                        s_vec.inc(v.scalar_tensor_tensor(
                            out_stage[:, oc * 512:(oc + 1) * 512],
                            ps_moe[:, oc * 512:(oc + 1) * 512],
                            gb_t[:, oc:oc + 1], xs, ALU.mult, ALU.add))
    moe_scope.close()

    with nc.Block("fin") as block:
        @block.sync
        def _(sy):
            sy.wait_ge(s_out.h, s_out.n)

    es.close()
    nc.finalize()
    return nc, sorted(tap_d.keys())


_cache = {}

def _get_program(taps=()):
    key = tuple(sorted(taps))
    if key not in _cache:
        _cache[key] = build_program(taps=key)
    return _cache[key]


def make_in_maps(inputs):
    f32 = np.float32
    def cm(w):
        w = np.asarray(w, f32)
        return np.ascontiguousarray(w.reshape(NCH, 128, w.shape[1]).transpose(1, 0, 2)
                                    .reshape(128, NCH * w.shape[1]))
    x = np.asarray(inputs["L0_attn_x"], f32)
    C = np.asarray(inputs["L0_c"], f32)
    wmod0 = np.asarray(inputs["L0_wmod"], f32).reshape(D, 6, D).transpose(1, 0, 2)
    wmod = np.ascontiguousarray(
        wmod0.reshape(6, NCH, 128, D).transpose(0, 2, 1, 3).reshape(6, 128, NCH * D))
    ew0 = np.asarray(inputs["L0_ffn_expert_w"], f32).transpose(0, 2, 1)
    ew = np.ascontiguousarray(
        ew0.reshape(NE, NCH, 128, D).transpose(0, 2, 1, 3).reshape(NE, 128, NCH * D))
    shared = {
        "wq": cm(inputs["L0_attn_Wq"]),
        "wk": cm(inputs["L0_attn_Wk"]),
        "wv": cm(inputs["L0_attn_Wv"]),
        "wf": cm(inputs["L0_attn_Wf"]),
        "wmod": wmod,
        "bqr": np.ascontiguousarray(np.asarray(inputs["L0_attn_bq"], f32).reshape(1, D)),
        "bkr": np.ascontiguousarray(np.asarray(inputs["L0_attn_bk"], f32).reshape(1, D)),
        "bvr": np.ascontiguousarray(np.asarray(inputs["L0_attn_bv"], f32).reshape(1, D)),
        "bfr": np.ascontiguousarray(np.asarray(inputs["L0_attn_bf"], f32).reshape(1, D)),
        "rwT": np.ascontiguousarray(np.asarray(inputs["L0_ffn_router_w"], f32).T),
        "rbr": np.ascontiguousarray(np.asarray(inputs["L0_ffn_router_b"], f32).reshape(1, NE)),
        "ew": ew,
        "eb": np.ascontiguousarray(np.asarray(inputs["L0_ffn_expert_b"], f32)),
    }
    in_maps = []
    for b in range(8):
        order = [b] + [i for i in range(8) if i != b]
        m = dict(shared)
        xb = x[b].T
        m["x"] = np.ascontiguousarray(xb.reshape(NCH, 128, S).transpose(1, 0, 2)
                                      .reshape(128, NCH * S))
        m["cT"] = np.ascontiguousarray(C[order].T)
        in_maps.append(m)
    return in_maps


def kernel(**inputs):
    nc, _ = _get_program()
    in_maps = make_in_maps(inputs)
    res = run_bass_kernel_spmd(nc, in_maps, core_ids=list(range(8)))
    out = np.stack([np.asarray(r["out"]).T for r in res.results], axis=0)
    return out.astype(np.float32)
